# version 15
# baseline (speedup 1.0000x reference)
import numpy as np
import ml_dtypes
import concourse.bass as bass
import concourse.mybir as mybir
from concourse.bass_utils import run_bass_kernel_spmd

F32 = mybir.dt.float32
BF16 = mybir.dt.bfloat16
U8 = mybir.dt.uint8
AF = mybir.ActivationFunctionType
ALU = mybir.AluOpType
AX = mybir.AxisListType
BFD = ml_dtypes.bfloat16

T_SEQ = 2048
DM = 1024
NH = 16
NTT = 16
NEG_BIG = -1.0e30
N_SLOT = 6

class _Rec:
    __slots__ = ("eng", "dma", "thunk", "deps", "marked", "value", "waits",
                 "seq", "tag", "lane", "lane_pred")

    def __init__(self, eng, dma, thunk, seq, tag):
        self.eng = eng
        self.dma = dma
        self.thunk = thunk
        self.deps = []
        self.marked = False
        self.value = None
        self.waits = None
        self.seq = seq
        self.tag = tag
        self.lane = None
        self.lane_pred = None


class _Buf:
    __slots__ = ("name", "writers", "readers")

    def __init__(self, name):
        self.name = name
        self.writers = []
        self.readers = []

    def absorb(self, others):
        for o in others:
            self.writers = self.writers + o.writers + o.readers


class _Sched:
    ENGINES = ("tensor", "vector", "scalar", "gpsimd", "sync")
    NLANE = 8

    def __init__(self):
        self.recs = {e: [] for e in self.ENGINES}
        self.nseq = 0
        self.dma_count = {e: 0 for e in self.ENGINES}
        self.lane_last = {}

    def op(self, eng, thunk, reads=(), writes=(), pwrites=(), dma=False, tag=""):
        r = _Rec(eng, dma, thunk, self.nseq, tag)
        self.nseq += 1
        if dma:
            lane = self.dma_count[eng] % self.NLANE
            self.dma_count[eng] += 1
            r.lane = lane
            r.lane_pred = self.lane_last.get((eng, lane))
            self.lane_last[(eng, lane)] = r
        deps = {}
        for b in reads:
            for d in b.writers:
                deps[id(d)] = d
        for b in tuple(writes) + tuple(pwrites):
            for d in b.writers:
                deps[id(d)] = d
            for d in b.readers:
                deps[id(d)] = d
        if dma and r.lane_pred is not None:
            deps[id(r.lane_pred)] = r.lane_pred
        for d in deps.values():
            if d is r:
                continue
            if (not d.dma) and (not dma) and d.eng == eng == "tensor":
                continue
            assert d.seq < r.seq, (d.tag, r.tag)
            r.deps.append(d)
        for b in reads:
            b.readers.append(r)
        for b in writes:
            b.writers = [r]
            b.readers = []
        for b in pwrites:
            b.writers.append(r)
        self.recs[eng].append(r)
        return r

    def finalize(self):
        for lst in self.recs.values():
            for r in lst:
                for d in r.deps:
                    d.marked = True
                if r.dma:
                    r.marked = True
        for lst in self.recs.values():
            cnt = {}
            for r in lst:
                if r.marked:
                    key = ("d", r.lane) if r.dma else "c"
                    cnt[key] = cnt.get(key, 0) + 1
                    r.value = cnt[key] * (16 if r.dma else 1)
        for lst in self.recs.values():
            hw = {}
            for r in lst:
                ws = []
                for d in r.deps:
                    key = (d.eng, d.lane) if d.dma else (d.eng, None)
                    assert d.value is not None
                    if hw.get(key, 0) < d.value:
                        ws.append((key, d.value))
                        hw[key] = d.value
                ws.sort(key=lambda kv: kv[1])
                r.waits = ws

    def fin(self, bufs, tag="fin"):
        r = self.op("sync", lambda e: e.nop(), reads=tuple(bufs), tag=tag)
        have = {id(d) for d in r.deps}
        for rec in self.lane_last.values():
            if id(rec) not in have:
                r.deps.append(rec)
                have.add(id(rec))
        return r

    def run(self, eng_name, eng, sems):
        for r in self.recs[eng_name]:
            for key, val in r.waits:
                eng.wait_ge(sems[key], val)
            ins = r.thunk(eng)
            if r.marked:
                assert ins is not None, r.tag
                key = (eng_name, r.lane) if r.dma else (eng_name, None)
                ins.then_inc(sems[key], 16 if r.dma else 1)


def _host_constants():
    c = {}
    c["ident_f32"] = np.eye(128, dtype=np.float32)
    c["ident_bf"] = np.eye(128, dtype=np.float32).astype(BFD)
    i_of_r = (np.arange(128) % 64) // 2
    freqs = np.exp(-np.log(10000.0) * i_of_r / 32.0)
    t = np.arange(T_SEQ, dtype=np.float64)
    ang = freqs[:, None] * t[None, :]
    cos = np.cos(ang)
    sin = np.sin(ang)
    sgn = np.where(np.arange(128) % 2 == 0, -1.0, 1.0)[:, None]
    c["Cq"] = (0.125 * cos).astype(np.float32)
    c["Sq"] = (0.125 * sgn * sin).astype(np.float32)
    c["Ck"] = cos.astype(np.float32)
    c["Sk"] = (sgn * sin).astype(np.float32)
    tq = np.arange(128)[:, None]
    cw = np.arange(384)[None, :]
    valid = (cw > tq + 6) & (cw <= tq + 256)
    c["maskT"] = np.where(valid, 0.0, NEG_BIG).astype(np.float32)
    c["ones_bf"] = np.ones((128, 1), np.float32).astype(BFD)
    c["neg1"] = np.full((128, 1), -1.0, np.float32)
    c["onesc2"] = np.full((1, 128), 2.0, np.float32)
    c["one11"] = np.ones((1, 1), np.float32)
    c["onesc1"] = np.ones((1, 128), np.float32)
    return c


_CONST_SPECS = [
    ("ident_f32", (128, 128), F32), ("ident_bf", (128, 128), BF16),
    ("Cq", (128, 2048), F32), ("Sq", (128, 2048), F32),
    ("Ck", (128, 2048), F32), ("Sk", (128, 2048), F32),
    ("maskT", (128, 384), F32), ("ones_bf", (128, 1), BF16),
    ("neg1", (128, 1), F32),
    ("onesc2", (1, 128), F32), ("one11", (1, 1), F32),
    ("onesc1", (1, 128), F32),
]


def build_nc(debug=(), stage=4):
    nc = bass.Bass("TRN2", target_bir_lowering=False, debug=False)
    sch = _Sched()

    dx = nc.dram_tensor("x", [T_SEQ, DM], F32, kind="ExternalInput")
    dwqk = nc.dram_tensor("wqk", [16, 128, 8, 128], F32, kind="ExternalInput")
    dwv = nc.dram_tensor("wv", [8, 128, 1024], F32, kind="ExternalInput")
    dwo = nc.dram_tensor("wo", [8, 128, 8, 128], F32, kind="ExternalInput")
    dk0a = nc.dram_tensor("k0a", [8, 128, 4096], U8, kind="ExternalInput")
    dk0b = nc.dram_tensor("k0b", [8, 128, 4096], U8, kind="ExternalInput")
    dk1 = nc.dram_tensor("k1", [32, 128, 4096], U8, kind="ExternalInput")
    dk2 = nc.dram_tensor("k2", [32, 128, 1024], U8, kind="ExternalInput")
    dsc = [nc.dram_tensor(f"s{j}", [1, 1], F32, kind="ExternalInput")
           for j in range(3)]
    dconst = {name: nc.dram_tensor("c_" + name, list(shp), dt, kind="ExternalInput")
              for name, shp, dt in _CONST_SPECS}
    dout = nc.dram_tensor("out", [T_SEQ, DM], F32, kind="ExternalOutput")
    dw0s = nc.dram_tensor("w0s", [8, 128, 4096], BF16, kind="Internal")
    dw1s = nc.dram_tensor("w1s", [32, 128, 4096], BF16, kind="Internal")
    dw2s = nc.dram_tensor("w2s", [32, 128, 1024], BF16, kind="Internal")

    ddbg = {}
    for name, shp, dt in [
        ("xn_f", [128, 16384], BF16), ("q_f", [128, 16384], BF16),
        ("k_f", [128, 16384], BF16), ("v_t", [128, 16384], BF16),
        ("a_t", [128, 16384], BF16),
        ("a_f", [128, 16384], BF16), ("s_t", [128, 16384], F32),
        ("t1", [128, 16384], BF16), ("t2", [128, 16384], BF16),
        ("w0s", [8, 128, 4096], BF16), ("w1s", [32, 128, 4096], BF16),
    ]:
        if name in debug:
            ddbg[name] = nc.dram_tensor("dbg_" + name, shp, dt,
                                        kind="ExternalOutput")

    sb_xn = nc.alloc_sbuf_tensor("xn_f", [128, 16384], BF16)
    sb_mask = nc.alloc_sbuf_tensor("maskT", [128, 384], F32)
    sb_idf = nc.alloc_sbuf_tensor("ident_f32", [128, 128], F32)
    sb_idb = nc.alloc_sbuf_tensor("ident_bf", [128, 128], BF16)
    sb_1bf = nc.alloc_sbuf_tensor("ones_bf", [128, 1], BF16)
    sb_neg1 = nc.alloc_sbuf_tensor("neg1", [128, 1], F32)
    sb_oc2 = nc.alloc_sbuf_tensor("onesc2", [1, 128], F32)
    sb_one11 = nc.alloc_sbuf_tensor("one11", [1, 1], F32)
    sb_oc1 = nc.alloc_sbuf_tensor("onesc1", [1, 128], F32)
    sb_sca = nc.alloc_sbuf_tensor("scalars", [1, 4], F32)
    sb_sb2 = nc.alloc_sbuf_tensor("s012b2", [128, 4], F32)
    sb_lnst = nc.alloc_sbuf_tensor("lnst", [128, 12], F32)
    sb_rec = nc.alloc_sbuf_tensor("rec", [128, N_SLOT], F32)
    sb_smx = nc.alloc_sbuf_tensor("softmax_sums", [128, N_SLOT], F32)
    sb_mng = nc.alloc_sbuf_tensor("mneg", [128, N_SLOT], F32)

    sb_st = nc.alloc_sbuf_tensor("s_t", [128, 16384], F32)
    a1 = nc.lookup_mloc(sb_st).addr
    sb_q = nc.alloc_sbuf_tensor_at("q_f", [128, 16384], BF16, offset=a1)
    sb_k = nc.alloc_sbuf_tensor_at("k_f", [128, 16384], BF16, offset=a1 + 32768)
    sb_vt = nc.alloc_sbuf_tensor("v_t", [128, 16384], BF16)
    a2 = nc.lookup_mloc(sb_vt).addr
    sb_af = nc.alloc_sbuf_tensor_at("a_f", [128, 16384], BF16, offset=a2)
    sb_t2 = nc.alloc_sbuf_tensor_at("t2", [128, 16384], BF16, offset=a2)
    sb_at = nc.alloc_sbuf_tensor("a_t", [128, 16384], BF16)
    a3 = nc.lookup_mloc(sb_at).addr
    sb_Cq = nc.alloc_sbuf_tensor_at("Cq", [128, 2048], F32, offset=a3)
    sb_Sq = nc.alloc_sbuf_tensor_at("Sq", [128, 2048], F32, offset=a3 + 8192)
    sb_Ck = nc.alloc_sbuf_tensor_at("Ck", [128, 2048], F32, offset=a3 + 16384)
    sb_Sk = nc.alloc_sbuf_tensor_at("Sk", [128, 2048], F32, offset=a3 + 24576)
    sb_t1 = nc.alloc_sbuf_tensor_at("t1", [128, 16384], BF16, offset=a3)
    R4 = 45056
    sb_r4 = nc.alloc_sbuf_tensor("R4", [128, R4], U8)
    a4 = nc.lookup_mloc(sb_r4).addr

    def r4(name, shape, dtype, off):
        nbytes = int(np.prod(shape[1:])) * mybir.dt.size(dtype)
        assert off + nbytes <= R4, (name, off, nbytes)
        return nc.alloc_sbuf_tensor_at(name, list(shape), dtype, offset=a4 + off)

    sb_xb = [r4(f"xbuf{i}", [128, 1024], F32, i * 4096) for i in range(2)]
    sb_wqs = [r4(f"wqkst{i}", [128, 1024], F32, 8192 + i * 4096) for i in range(2)]
    sb_sqd = r4("sqdump", [128, 1024], F32, 12288)
    sb_wqb = [r4(f"wqkbf{i}", [128, 1024], BF16, 16384 + i * 2048) for i in range(2)]
    sb_wvb = r4("wv_bf", [128, 8192], BF16, 20480)
    sb_wvs = r4("wv_st", [128, 1024], F32, 36864)
    sb_swp = [r4("swp", [128, 512], F32, 40960),
              r4("swp1", [128, 512], F32, 36864)]
    sb_tr1 = [r4("tr1", [128, 512], F32, 43008),
              r4("tr11", [128, 512], F32, 38912)]
    sb_cvU = [r4(f"cvU{i}", [128, 2048], U8, i * 2048) for i in range(2)]
    sb_cvO = [r4(f"cvO{i}", [128, 2048], BF16, 4096 + i * 4096) for i in range(2)]
    sb_cvC = [r4(f"cvC{i}", [128, 2048], U8, 12288 + i * 2048) for i in range(2)]
    sb_T1 = r4("T1", [128, N_SLOT * 384], F32, 16384)
    sb_Pe = r4("Pe", [128, N_SLOT * 384], BF16, 25600)
    sb_PeT = r4("PeT", [128, N_SLOT * 384], BF16, 30208)
    sb_slab = [r4(f"slab{i}", [128, 6144], BF16, i * 6144) for i in range(3)]
    sb_gb = [r4(f"gbuf{i}", [128, 512], F32, 18432 + i * 2048) for i in range(2)]
    sb_pb = [r4(f"pbuf{i}", [128, 512], F32, 22528 + i * 2048) for i in range(2)]
    sb_ob = [r4(f"outb{i}", [128, 1024], F32, 26624 + i * 4096) for i in range(4)]

    psf = [nc.alloc_psum_tensor(f"psf{i}", [128, 512], F32) for i in range(6)]
    psb = [nc.alloc_psum_tensor(f"psb{i}", [128, 1024], BF16) for i in range(2)]

    B = _Buf
    b_const = B("const")
    b_cs = B("cs_tables")
    b_sca = B("sca")
    b_sb2 = B("sb2")
    b_xb = [B(f"xb{i}") for i in range(2)]
    b_lnst = [B(f"lnst{i}") for i in range(2)]
    b_sqd = B("sqd")
    b_xnf = [[B(f"xnf{c}_{q}") for q in range(4)] for c in range(8)]
    b_wqs = [B(f"wqs{i}") for i in range(2)]
    b_wqb = [B(f"wqb{i}") for i in range(2)]
    b_wvs = B("wvs")
    b_wvb = B("wvb")
    b_q = [B(f"q{c}") for c in range(8)]
    b_k = [B(f"k{c}") for c in range(8)]
    b_vt = [B(f"vt{t}") for t in range(NTT)]
    b_swp = [B("swp0"), B("swp1")]
    b_tr1 = [B("tr10"), B("tr11")]
    b_psf = [B(f"psf{i}") for i in range(6)]
    b_psb = [B(f"psb{i}") for i in range(2)]
    b_T1 = [B(f"T1_{s}") for s in range(N_SLOT)]
    b_Pe = [B(f"Pe_{s}") for s in range(N_SLOT)]
    b_PeT = [B(f"PeT_{s}") for s in range(N_SLOT)]
    b_smx = [B(f"smx{s}") for s in range(N_SLOT)]
    b_mng = [B(f"mng{s}") for s in range(N_SLOT)]
    b_recsb = [B(f"recsb{s}") for s in range(N_SLOT)]
    b_at = [B(f"at{t}") for t in range(NTT)]
    b_af = [B(f"af{c}") for c in range(8)]
    b_st = [B(f"st{t}") for t in range(NTT)]
    b_cvU = [B(f"cvU{i}") for i in range(2)]
    b_cvO = [B(f"cvO{i}") for i in range(2)]
    b_cvC = [B(f"cvC{i}") for i in range(2)]
    b_w0s = [B(f"w0s{c}") for c in range(8)]
    b_w1s = [B(f"w1s{c}") for c in range(32)]
    b_w2s = [B(f"w2s{c}") for c in range(32)]

    b_slab = [B(f"slab{i}") for i in range(3)]
    b_t1 = [B(f"t1_{m}") for m in range(32)]
    b_t2 = [B(f"t2_{m}") for m in range(32)]
    b_gb = [B(f"gb{i}") for i in range(2)]
    b_pb = [B(f"pb{i}") for i in range(2)]
    b_ob = [B(f"ob{i}") for i in range(4)]
    b_out = B("out_dram")
    b_dbg = B("dbg")

    def dma(q, dst, src, reads=(), writes=(), pwrites=(), tag=""):
        return sch.op(q, lambda e, d=dst, s=src: e.dma_start(d, s),
                      reads=reads, writes=writes, pwrites=pwrites, dma=True,
                      tag=tag)

    def mm(out, lhsT, rhs, start, stop, reads=(), writes=(), pwrites=(), tag=""):
        return sch.op("tensor",
                      lambda e, o=out, l=lhsT, r=rhs, st=start, sp=stop:
                      e.matmul(o, l, r, start=st, stop=sp),
                      reads=reads, writes=writes, pwrites=pwrites, tag=tag)

    def transp(out, inp, ident, reads=(), writes=(), pwrites=(), tag=""):
        return sch.op("tensor",
                      lambda e, o=out, i=inp, d=ident: e.transpose(o, i, d),
                      reads=reads, writes=writes, pwrites=pwrites, tag=tag)

    sb_konst = dict(ident_f32=sb_idf, ident_bf=sb_idb, Cq=sb_Cq, Sq=sb_Sq,
                    Ck=sb_Ck, Sk=sb_Sk, maskT=sb_mask, ones_bf=sb_1bf,
                    neg1=sb_neg1, onesc2=sb_oc2, one11=sb_one11, onesc1=sb_oc1)

    def ldconst(name):
        buf = b_cs if name[0] in "CS" and name != "maskT" else b_const
        dma("sync", sb_konst[name][:], dconst[name].ap(),
            pwrites=(buf,), tag=f"ld_{name}")

    for t in range(2):
        dma("sync", sb_xb[t][:], dx.ap()[t * 128:(t + 1) * 128, :],
            writes=(b_xb[t],), tag=f"ldx{t}")
    ldconst("ident_f32")
    for kc in range(8):
        dma("sync", sb_wvs[:], dwv.ap()[kc], writes=(b_wvs,), tag=f"ldwv{kc}")
        sch.op("scalar",
               lambda e, kc=kc: e.copy(sb_wvb[:, kc * 1024:(kc + 1) * 1024],
                                       sb_wvs[:]),
               reads=(b_wvs,), pwrites=(b_wvb,), tag=f"cwv{kc}")
    for m in range(2):
        dma("sync", sb_wqs[m][:], dwqk.ap()[m], writes=(b_wqs[m],),
            tag=f"ldwqk{m}")
        sch.op("scalar", lambda e, m=m: e.copy(sb_wqb[m][:], sb_wqs[m][:]),
               reads=(b_wqs[m],), writes=(b_wqb[m],), tag=f"cwqk{m}")
    for name, _, _dt in _CONST_SPECS:
        if name != "ident_f32":
            ldconst(name)
    for j in range(3):
        dma("sync", sb_sca[0:1, j:j + 1], dsc[j].ap(), pwrites=(b_sca,),
            tag=f"ld_s{j}")
    for j in range(3):
        mm(psf[5][:, j:j + 1], sb_oc2[:, :], sb_sca[0:1, j:j + 1], True, True,
           reads=(b_const, b_sca),
           writes=(b_psf[5],) if j == 0 else (), pwrites=(b_psf[5],) if j else (),
           tag=f"bc_s{j}")
    sch.op("vector", lambda e: e.tensor_copy(sb_sb2[:, 0:3], psf[5][:, 0:3]),
           reads=(b_psf[5],), writes=(b_sb2,), tag="ev_s012")
    b_sqd.absorb([b_wqs[1]])

    xnf_r = sb_xn[:].rearrange("p (c n) -> p c n", c=8)
    for t in range(NTT):
        xb = sb_xb[t % 2]
        bxb = b_xb[t % 2]
        sl = t % 2
        bst = b_lnst[sl]
        st_c = sb_lnst[:, sl * 6: sl * 6 + 6]
        xs, mu, vs, v1, ln1, inv = (st_c[:, i:i + 1] for i in range(6))
        if t >= 2:
            dma("sync", xb[:], dx.ap()[t * 128:(t + 1) * 128, :], writes=(bxb,),
                tag=f"ldx{t}")
        sch.op("vector", lambda e, xb=xb, xs=xs: e.reduce_sum(
            xs, xb[:], axis=AX.X),
               reads=(bxb,), writes=(bst,), tag=f"ln_sum{t}")
        sch.op("vector", lambda e, mu=mu, xs=xs: e.tensor_scalar_mul(
            mu, xs, 1.0 / DM),
               reads=(bst,), pwrites=(bst,), tag=f"ln_mu{t}")
        sch.op("vector", lambda e, xb=xb, mu=mu: e.tensor_scalar(
            xb[:], xb[:], mu, None, op0=ALU.subtract),
               reads=(bst,), writes=(bxb,), tag=f"ln_sub{t}")
        sch.op("scalar", lambda e, xb=xb, vs=vs: e.activation(
            sb_sqd[:], xb[:], AF.Square, accum_out=vs),
               reads=(bxb,), writes=(b_sqd,), pwrites=(bst,), tag=f"ln_sq{t}")
        sch.op("vector", lambda e, v1=v1, vs=vs: e.tensor_scalar(
            v1, vs, 1.0 / DM, 1e-5, op0=ALU.mult, op1=ALU.add),
               reads=(bst,), pwrites=(bst,), tag=f"ln_v1{t}")
        sch.op("scalar", lambda e, ln1=ln1, v1=v1: e.activation(
            ln1, v1, AF.Ln),
               reads=(bst,), pwrites=(bst,), tag=f"ln_ln{t}")
        sch.op("scalar", lambda e, inv=inv, ln1=ln1: e.activation(
            inv, ln1, AF.Exp, scale=-0.5),
               reads=(bst,), pwrites=(bst,), tag=f"ln_inv{t}")
        sch.op("vector", lambda e, xb=xb, inv=inv: e.tensor_scalar_mul(
            xb[:], xb[:], inv),
               reads=(bst,), writes=(bxb,), tag=f"ln_fin{t}")
        for g in range(2):
            bank = psf[4 + g]
            bb = b_psf[4 + g]
            for qq in range(4):
                cchunk = g * 4 + qq
                transp(bank[:, qq * 128:(qq + 1) * 128],
                       xb[:, cchunk * 128:(cchunk + 1) * 128], sb_idf[:],
                       reads=(bxb, b_const),
                       writes=(bb,) if qq == 0 else (),
                       pwrites=(bb,) if qq else (),
                       tag=f"tp{t}_{cchunk}")
            dst = xnf_r[:, g * 4:(g + 1) * 4, t * 128:(t + 1) * 128]
            src = bank[:].rearrange("p (c n) -> p c n", c=4)
            sch.op("scalar", lambda e, d=dst, s=src: e.copy(d, s),
                   reads=(bb,),
                   pwrites=tuple(b_xnf[g * 4 + c][t // 4] for c in range(4)),
                   tag=f"tpev{t}_{g}")
    b_wqs[1].absorb([b_sqd])

    swmask = []
    for i in range(16):
        swmask += [2 * i + 1, 2 * i]
    b_swp[1].absorb([b_wvs])
    b_tr1[1].absorb([b_wvs])
    for m in range(16):
        stg, bstg = sb_wqs[m % 2], b_wqs[m % 2]
        wb, bwb = sb_wqb[m % 2], b_wqb[m % 2]
        if m >= 2:
            dma("sync", stg[:], dwqk.ap()[m], writes=(bstg,), tag=f"ldwqk{m}")
            sch.op("scalar", lambda e, wb=wb, stg=stg: e.copy(wb[:], stg[:]),
                   reads=(bstg,), writes=(bwb,), tag=f"cwqk{m}")
        isq = m < 8
        ctab, stab = (sb_Cq, sb_Sq) if isq else (sb_Ck, sb_Sk)
        dstt = sb_q if isq else sb_k
        bdst = b_q[m] if isq else b_k[m - 8]
        chunk = m if isq else m - 8
        for nt in range(4):
            bank, bb = psf[nt], b_psf[nt]
            for kc in range(8):
                mm(bank[:, 0:512], wb[:, kc * 128:(kc + 1) * 128],
                   sb_xn[:, kc * 2048 + nt * 512: kc * 2048 + nt * 512 + 512],
                   kc == 0, kc == 7,
                   reads=(bwb, b_xnf[kc][nt]),
                   writes=(bb,) if kc == 0 else (), pwrites=(bb,) if kc else (),
                   tag=f"qk{m}_{nt}_{kc}")
            cs = slice(nt * 512, nt * 512 + 512)
            rsl = (m * 4 + nt) % 2
            swp, bswp = sb_swp[rsl], b_swp[rsl]
            tr1, btr1 = sb_tr1[rsl], b_tr1[rsl]
            sch.op("vector", lambda e, bank=bank, swp=swp: e.stream_shuffle(
                swp[:], bank[:, 0:512], swmask),
                   reads=(bb,), writes=(bswp,), tag=f"rp_sh{m}_{nt}")
            sch.op("vector", lambda e, bank=bank, ctab=ctab, cs=cs, tr1=tr1:
                   e.tensor_mul(tr1[:], bank[:, 0:512], ctab[:, cs]),
                   reads=(bb, b_cs), writes=(btr1,), tag=f"rp_c{m}_{nt}")
            sch.op("vector", lambda e, stab=stab, cs=cs, swp=swp: e.tensor_mul(
                swp[:], swp[:], stab[:, cs]),
                   reads=(bswp, b_cs), writes=(bswp,), tag=f"rp_s{m}_{nt}")
            dst = dstt[:, chunk * 2048 + nt * 512: chunk * 2048 + nt * 512 + 512]
            sch.op("vector", lambda e, d=dst, tr1=tr1, swp=swp: e.tensor_add(
                d, tr1[:], swp[:]),
                   reads=(btr1, bswp), pwrites=(bdst,), tag=f"rp_o{m}_{nt}")
    for tt in range(NTT):
        for og in range(2):
            bank = psf[4 + ((tt * 2 + og) % 2)]
            bb = b_psf[4 + ((tt * 2 + og) % 2)]
            for kc in range(8):
                mm(bank[:, 0:512],
                   sb_xn[:, kc * 2048 + tt * 128: kc * 2048 + tt * 128 + 128],
                   sb_wvb[:, kc * 1024 + og * 512: kc * 1024 + og * 512 + 512],
                   kc == 0, kc == 7,
                   reads=(b_xnf[kc][tt // 4], b_wvb),
                   writes=(bb,) if kc == 0 else (), pwrites=(bb,) if kc else (),
                   tag=f"v{tt}_{og}_{kc}")
            dst = sb_vt[:, tt * 1024 + og * 512: tt * 1024 + og * 512 + 512]
            sch.op("scalar", lambda e, d=dst, bank=bank: e.copy(d, bank[:, 0:512]),
                   reads=(bb,), pwrites=(b_vt[tt],), tag=f"vev{tt}_{og}")

    if "xn_f" in ddbg:
        dma("sync", ddbg["xn_f"].ap(), sb_xn[:], reads=tuple(b for row in b_xnf for b in row), pwrites=(b_dbg,), tag="dbg_xn")
    if "q_f" in ddbg:
        dma("sync", ddbg["q_f"].ap(), sb_q[:], reads=tuple(b_q), pwrites=(b_dbg,), tag="dbg_q")
    if "k_f" in ddbg:
        dma("sync", ddbg["k_f"].ap(), sb_k[:], reads=tuple(b_k), pwrites=(b_dbg,), tag="dbg_k")
    if "v_t" in ddbg:
        dma("sync", ddbg["v_t"].ap(), sb_vt[:], reads=tuple(b_vt), pwrites=(b_dbg,), tag="dbg_v")

    if stage < 2:
        sch.fin([b_dbg])
        return _emit(nc, sch)
    b_cvU[0].absorb([b_xb[0]])
    b_cvU[1].absorb([b_xb[0]])
    b_cvO[0].absorb([b_xb[1]])
    b_cvO[1].absorb([b_wqs[0]])
    b_cvC[0].absorb([b_wqs[1], b_sqd])
    b_cvC[1].absorb([b_wqs[1], b_sqd])
    for s in range(N_SLOT):
        b_T1[s].absorb([b_wqb[0], b_wqb[1], b_wvb])
        b_Pe[s].absorb([b_wvb])
        b_PeT[s].absorb([b_wvb])

    conv_steps = []

    def conv_half(src_ap, dst_ap, wbuf, idx, src2_ap=None, tagn="", n=2048):
        ia, ic = idx % 2, idx % 2

        def s1():
            dma("sync", sb_cvU[ia][:, 0:n], src_ap, writes=(b_cvU[ia],),
                tag=f"cva{tagn}")

        def s1b():
            dma("sync", sb_cvC[ic][:, 0:n], src2_ap, writes=(b_cvC[ic],),
                tag=f"cvb{tagn}")

        def s2sub():
            sch.op("vector", lambda e: e.tensor_sub(
                sb_cvO[ia][:, 0:n], sb_cvU[ia][:, 0:n], sb_cvC[ic][:, 0:n]),
                   reads=(b_cvU[ia], b_cvC[ic]), writes=(b_cvO[ia],),
                   tag=f"cvs{tagn}")

        def s2aff():
            sch.op("vector", lambda e: e.tensor_scalar(
                sb_cvO[ia][:, 0:n], sb_cvU[ia][:, 0:n], 2.0, -1.0,
                op0=ALU.mult, op1=ALU.add),
                   reads=(b_cvU[ia],), writes=(b_cvO[ia],), tag=f"cvf{tagn}")

        def s3():
            dma("sync", dst_ap, sb_cvO[ia][:, 0:n], reads=(b_cvO[ia],),
                writes=(wbuf,), tag=f"cvw{tagn}")
        if src2_ap is not None:
            return [s1, s1b, s2sub, s3]
        return [s1, s2aff, s3]

    ci = 0
    for kc in range(8):
        for h in range(2):
            conv_steps.extend(conv_half(
                dk0a.ap()[kc, :, h * 2048:(h + 1) * 2048],
                dw0s.ap()[kc, :, h * 2048:(h + 1) * 2048],
                b_w0s[kc], ci,
                src2_ap=dk0b.ap()[kc, :, h * 2048:(h + 1) * 2048],
                tagn=f"w0_{kc}_{h}"))
            ci += 1
    for kc in range(32):
        for h in range(2):
            conv_steps.extend(conv_half(
                dk1.ap()[kc, :, h * 2048:(h + 1) * 2048],
                dw1s.ap()[kc, :, h * 2048:(h + 1) * 2048],
                b_w1s[kc], ci, tagn=f"w1_{kc}_{h}"))
            ci += 1
    for kc in range(32):
        conv_steps.extend(conv_half(
            dk2.ap()[kc], dw2s.ap()[kc], b_w2s[kc], ci, tagn=f"w2_{kc}", n=1024))
        ci += 1
    conv_iter = iter(conv_steps)

    def emit_conv(k):
        for _ in range(k):
            step = next(conv_iter, None)
            if step is None:
                return
            step()

    for t in range(NTT):
        b_at[t].absorb([b_cs])
    units = [(i, h) for i in range(NTT) for h in range(NH)]
    NU = len(units)

    def uctx(u):
        i, h = units[u]
        slot = u % N_SLOT
        jj0 = max(0, 2 - i)
        return dict(i=i, h=h, slot=slot, bank=psf[slot], bbank=b_psf[slot],
                    pbank=psb[u % 2], bpbank=b_psb[u % 2],
                    c=h // 2, off=(h % 2) * 64, jj0=jj0,
                    js=list(range(jj0, 3)))

    def st_mm(u):
        t = uctx(u)
        i, h, c, off, jj0 = t["i"], t["h"], t["c"], t["off"], t["jj0"]
        w = (3 - jj0) * 128
        k0c = (i - 2 + jj0) * 128
        mm(t["bank"][:, jj0 * 128:384],
           sb_q[off:off + 64, c * 2048 + i * 128: c * 2048 + i * 128 + 128],
           sb_k[off:off + 64, c * 2048 + k0c: c * 2048 + k0c + w],
           True, True, reads=(b_k[c], b_q[c]), writes=(t["bbank"],),
           tag=f"sq{i}_{h}")

    def st_mskmx(u):
        t = uctx(u)
        i, h, slot, jj0 = t["i"], t["h"], t["slot"], t["jj0"]
        t1s = sb_T1[:, slot * 384 + jj0 * 128: slot * 384 + 384]
        sch.op("vector", lambda e, bank=t["bank"], t1s=t1s, jj0=jj0: e.tensor_add(
            t1s, bank[:, jj0 * 128:384], sb_mask[:, jj0 * 128:384]),
               reads=(t["bbank"], b_const), writes=(b_T1[slot],), tag=f"msk{i}_{h}")
        sch.op("vector", lambda e, t1s=t1s, slot=slot: e.reduce_max(
            sb_mng[:, slot:slot + 1], t1s, axis=AX.X, negate=True),
               reads=(b_T1[slot],), writes=(b_mng[slot],), tag=f"mx{i}_{h}")

    def st_exp(u):
        t = uctx(u)
        i, h, slot, jj0 = t["i"], t["h"], t["slot"], t["jj0"]
        t1s = sb_T1[:, slot * 384 + jj0 * 128: slot * 384 + 384]
        pes = sb_Pe[:, slot * 384 + jj0 * 128: slot * 384 + 384]
        sch.op("scalar", lambda e, t1s=t1s, pes=pes, slot=slot: e.activation(
            pes, t1s, AF.Exp, bias=sb_mng[:, slot:slot + 1],
            accum_out=sb_smx[:, slot:slot + 1]),
               reads=(b_T1[slot], b_mng[slot]),
               writes=(b_Pe[slot], b_smx[slot]), tag=f"exp{i}_{h}")

    def st_tp(u):
        t = uctx(u)
        i, h, slot, jj0 = t["i"], t["h"], t["slot"], t["jj0"]
        for jj in t["js"]:
            transp(t["pbank"][:, jj * 128:(jj + 1) * 128],
                   sb_Pe[:, slot * 384 + jj * 128: slot * 384 + jj * 128 + 128],
                   sb_idb[:],
                   reads=(b_Pe[slot], b_const),
                   writes=(t["bpbank"],) if jj == jj0 else (),
                   pwrites=(t["bpbank"],) if jj != jj0 else (),
                   tag=f"pt{i}_{h}_{jj}")

    def st_rcp(u):
        t = uctx(u)
        i, h, slot = t["i"], t["h"], t["slot"]
        sch.op("vector", lambda e, slot=slot: e.reciprocal(
            sb_rec[:, slot:slot + 1], sb_smx[:, slot:slot + 1]),
               reads=(b_smx[slot],), writes=(b_recsb[slot],), tag=f"rcp{i}_{h}")

    def st_ptev(u):
        t = uctx(u)
        i, h, slot, jj0 = t["i"], t["h"], t["slot"], t["jj0"]
        pets = sb_PeT[:, slot * 384 + jj0 * 128: slot * 384 + 384]
        sch.op("scalar", lambda e, pets=pets, pbank=t["pbank"], jj0=jj0: e.copy(
            pets, pbank[:, jj0 * 128:384]),
               reads=(t["bpbank"],), writes=(b_PeT[slot],), tag=f"ptev{i}_{h}")

    def st_av(u):
        t = uctx(u)
        i, h, slot, jj0 = t["i"], t["h"], t["slot"], t["jj0"]
        for jj in t["js"]:
            jt = i - 2 + jj
            mm(t["bank"][:, 384:448],
               sb_PeT[:, slot * 384 + jj * 128: slot * 384 + jj * 128 + 128],
               sb_vt[:, jt * 1024 + h * 64: jt * 1024 + h * 64 + 64],
               jj == jj0, jj == 2,
               reads=(b_PeT[slot], b_vt[jt]), pwrites=(t["bbank"],),
               tag=f"av{i}_{h}_{jj}")

    def st_aev(u):
        t = uctx(u)
        i, h, slot = t["i"], t["h"], t["slot"]
        dst = sb_at[:, i * 1024 + h * 64: i * 1024 + h * 64 + 64]
        sch.op("scalar",
               lambda e, d=dst, bank=t["bank"], slot=slot: e.activation(
                   d, bank[:, 384:448], AF.Copy, scale=sb_rec[:, slot:slot + 1]),
               reads=(t["bbank"], b_recsb[slot]), pwrites=(b_at[i],),
               tag=f"aev{i}_{h}")

    stages = [(6, st_aev), (5, st_av), (4, st_ptev), (3, st_rcp), (3, st_tp),
              (2, st_exp), (1, st_mskmx), (0, st_mm)]
    per_it = len(conv_steps) // (NU + 7) + 1
    for it in range(NU + 7):
        emit_conv(per_it)
        for lag, fn in stages:
            u = it - lag
            if 0 <= u < NU:
                fn(u)
    emit_conv(10 ** 9)

    if "a_t" in ddbg:
        dma("sync", ddbg["a_t"].ap(), sb_at[:], reads=tuple(b_at),
            pwrites=(b_dbg,), tag="dbg_at")
    if stage < 3:
        sch.fin([b_dbg])
        return _emit(nc, sch)

    for c in range(8):
        b_af[c].absorb([b_vt[2 * c], b_vt[2 * c + 1]])
    for c in range(8):
        for half in range(2):
            bank, bb = psb[half], b_psb[half]
            for t8 in range(8):
                tt = half * 8 + t8
                transp(bank[:, t8 * 128:(t8 + 1) * 128],
                       sb_at[:, tt * 1024 + c * 128: tt * 1024 + c * 128 + 128],
                       sb_idb[:],
                       reads=(b_at[tt], b_const),
                       writes=(bb,) if t8 == 0 else (),
                       pwrites=(bb,) if t8 else (),
                       tag=f"atp{c}_{tt}")
            dst = sb_af[:, c * 2048 + half * 1024: c * 2048 + half * 1024 + 1024]
            sch.op("vector", lambda e, d=dst, bank=bank: e.tensor_copy(d, bank[:]),
                   reads=(bb,), pwrites=(b_af[c],), tag=f"atev{c}_{half}")

    if "a_f" in ddbg:
        dma("sync", ddbg["a_f"].ap(), sb_af[:], reads=tuple(b_af), pwrites=(b_dbg,), tag="dbg_af")

    for c8 in range(8):
        b_st[c8].absorb([b_q[c8], b_k[c8]])
    b_wqs[0].absorb([b_cvO[1]])
    b_wqs[1].absorb([b_cvC[0], b_cvC[1]])
    for mo in range(8):
        stg, bstg = sb_wqs[mo % 2], b_wqs[mo % 2]
        wb, bwb = sb_wqb[mo % 2], b_wqb[mo % 2]
        dma("sync", stg[:], dwo.ap()[mo], writes=(bstg,), tag=f"ldwo{mo}")
        sch.op("scalar", lambda e, wb=wb, stg=stg: e.copy(wb[:], stg[:]),
               reads=(bstg,), writes=(bwb,), tag=f"cwo{mo}")
        for nt in range(4):
            bank = psf[(mo * 4 + nt) % 4]
            bb = b_psf[(mo * 4 + nt) % 4]
            for kc in range(8):
                mm(bank[:, 0:512], wb[:, kc * 128:(kc + 1) * 128],
                   sb_af[:, kc * 2048 + nt * 512: kc * 2048 + nt * 512 + 512],
                   kc == 0, kc == 7,
                   reads=(bwb, b_af[kc]),
                   writes=(bb,) if kc == 0 else (), pwrites=(bb,) if kc else (),
                   tag=f"op{mo}_{nt}_{kc}")
            dst = sb_st[:, mo * 2048 + nt * 512: mo * 2048 + nt * 512 + 512]
            sch.op("vector", lambda e, d=dst, bank=bank: e.tensor_copy(
                d, bank[:, 0:512]),
                   reads=(bb,), pwrites=(b_st[mo],), tag=f"opev{mo}_{nt}")

    if "s_t" in ddbg:
        dma("sync", ddbg["s_t"].ap(), sb_st[:], reads=tuple(b_st[:8]), pwrites=(b_dbg,), tag="dbg_s")
    if "w0s" in ddbg:
        dma("sync", ddbg["w0s"].ap(), dw0s.ap(), reads=tuple(b_w0s), pwrites=(b_dbg,), tag="dbg_w0s")
    if "w1s" in ddbg:
        dma("sync", ddbg["w1s"].ap(), dw1s.ap(), reads=tuple(b_w1s), pwrites=(b_dbg,), tag="dbg_w1s")

    if stage < 4:
        sch.fin([b_dbg])
        return _emit(nc, sch)

    for m in range(32):
        b_t1[m].absorb([b_at[m // 2]])
        b_t2[m].absorb([b_af[m // 4]])
    for i2 in range(3):
        b_slab[i2].absorb([b_cvU[0], b_cvU[1], b_cvO[0], b_cvO[1],
                           b_cvC[0], b_cvC[1],
                           b_wqs[0], b_wqs[1], b_wqb[0], b_wqb[1]] + b_T1)
    for i2 in range(2):
        b_gb[i2].absorb([b_T1[4], b_T1[5], b_wqb[0], b_wqb[1]])
        b_pb[i2].absorb(b_T1 + b_Pe)
    for i2 in range(4):
        b_ob[i2].absorb(b_Pe + b_PeT + [b_wvs] + b_swp + b_tr1)

    def rounds(total, size):
        out = []
        m0 = 0
        while m0 < total:
            out.append(list(range(m0, min(m0 + size, total))))
            m0 += size
        return out

    slab_i = [0]

    def next_slab():
        slab_i[0] = (slab_i[0] + 1) % 3
        return sb_slab[slab_i[0]], b_slab[slab_i[0]]

    bank_set = [0]

    def next_banks(n):
        bank_set[0] ^= 1
        base = bank_set[0] * 3
        return [base + j for j in range(n)]

    def f0_round_emitters(nt):
        outs = []
        for rnd in rounds(32, 3):
            def em(rnd=rnd, nt=nt):
                w = len(rnd)
                banks = next_banks(w)
                slab, bslab = next_slab()
                src = dw0s.ap()[:, :, rnd[0] * 128: rnd[0] * 128 + w * 128]
                src = src.rearrange("c p f -> p c f")
                dst = slab[:, 0:8 * w * 128].rearrange("p (c f) -> p c f", c=8)
                dma("sync", dst, src, reads=tuple(b_w0s), writes=(bslab,),
                    tag=f"f0ld{nt}_{rnd[0]}")
                for kc in range(8):
                    for widx, m1 in enumerate(rnd):
                        bk = banks[widx]
                        mm(psf[bk][:, 0:512],
                           slab[:, kc * w * 128 + widx * 128:
                                kc * w * 128 + widx * 128 + 128],
                           sb_xn[:, kc * 2048 + nt * 512: kc * 2048 + nt * 512 + 512],
                           kc == 0, kc == 7,
                           reads=(bslab, b_xnf[kc][nt]),
                           writes=(b_psf[bk],) if kc == 0 else (),
                           pwrites=(b_psf[bk],) if kc else (),
                           tag=f"f0mm{nt}_{m1}_{kc}")
                for widx, m1 in enumerate(rnd):
                    bk = banks[widx]
                    dst2 = sb_t1[:, m1 * 512:(m1 + 1) * 512]
                    sch.op("vector", lambda e, d=dst2, bk=bk: e.tensor_scalar(
                        d, psf[bk][:, 0:512], sb_sb2[:, 0:1], 0.0,
                        op0=ALU.mult, op1=ALU.max),
                           reads=(b_psf[bk], b_sb2), writes=(b_t1[m1],),
                           tag=f"f0ev{nt}_{m1}")
            outs.append(em)
        return outs

    def f1_emit(nt):
        for rnd in rounds(32, 3):
            w = len(rnd)
            banks = next_banks(w)
            for g in range(4):
                slab, bslab = next_slab()
                src = dw1s.ap()[g * 8:(g + 1) * 8, :,
                                rnd[0] * 128: rnd[0] * 128 + w * 128]
                src = src.rearrange("c p f -> p c f")
                dst = slab[:, 0:8 * w * 128].rearrange("p (c f) -> p c f", c=8)
                dma("sync", dst, src,
                    reads=tuple(b_w1s[g * 8:(g + 1) * 8]), writes=(bslab,),
                    tag=f"f1ld{nt}_{rnd[0]}_{g}")
                for kci in range(8):
                    kc = g * 8 + kci
                    for widx, m2 in enumerate(rnd):
                        bk = banks[widx]
                        mm(psf[bk][:, 0:512],
                           slab[:, kci * w * 128 + widx * 128:
                                kci * w * 128 + widx * 128 + 128],
                           sb_t1[:, kc * 512:(kc + 1) * 512],
                           kc == 0, kc == 31,
                           reads=(bslab, b_t1[kc]),
                           writes=(b_psf[bk],) if kc == 0 else (),
                           pwrites=(b_psf[bk],) if kc else (),
                           tag=f"f1mm{nt}_{m2}_{kc}")
            for widx, m2 in enumerate(rnd):
                bk = banks[widx]
                dst = sb_t2[:, m2 * 512:(m2 + 1) * 512]
                sch.op("vector", lambda e, d=dst, bk=bk: e.tensor_scalar(
                    d, psf[bk][:, 0:512], sb_sb2[:, 1:2], 0.0,
                    op0=ALU.mult, op1=ALU.max),
                       reads=(b_psf[bk], b_sb2), writes=(b_t2[m2],),
                       tag=f"f1ev{nt}_{m2}")

    def f2_round_emitters(nt):
        outs = []

        def em_pre(nt=nt):
            for j in range(4):
                gt = nt * 4 + j
                dma("sync", sb_ob[j][:], dx.ap()[gt * 128:(gt + 1) * 128, :],
                    writes=(b_ob[j],), tag=f"ldxo{nt}_{j}")
        outs.append(em_pre)
        for rnd in rounds(8, 3):
            def em(rnd=rnd, nt=nt):
                w = len(rnd)
                banks = next_banks(w)
                for g in range(4):
                    slab, bslab = next_slab()
                    src = dw2s.ap()[g * 8:(g + 1) * 8, :,
                                    rnd[0] * 128: rnd[0] * 128 + w * 128]
                    src = src.rearrange("c p f -> p c f")
                    dst = slab[:, 0:8 * w * 128].rearrange("p (c f) -> p c f", c=8)
                    dma("sync", dst, src,
                        reads=tuple(b_w2s[g * 8:(g + 1) * 8]), writes=(bslab,),
                        tag=f"f2ld{nt}_{rnd[0]}_{g}")
                    for kci in range(8):
                        kc = g * 8 + kci
                        for widx, m3 in enumerate(rnd):
                            bk = banks[widx]
                            mm(psf[bk][:, 0:512],
                               slab[:, kci * w * 128 + widx * 128:
                                    kci * w * 128 + widx * 128 + 128],
                               sb_t2[:, kc * 512:(kc + 1) * 512],
                               kc == 0, kc == 31,
                               reads=(bslab, b_t2[kc]),
                               writes=(b_psf[bk],) if kc == 0 else (),
                               pwrites=(b_psf[bk],) if kc else (),
                               tag=f"f2mm{nt}_{m3}_{kc}")
                for widx, m3 in enumerate(rnd):
                    bk = banks[widx]
                    gb, bgb = sb_gb[m3 % 2], b_gb[m3 % 2]
                    pb, bpb = sb_pb[m3 % 2], b_pb[m3 % 2]
                    sch.op("scalar", lambda e, gb=gb, bk=bk: e.activation(
                        gb[:], psf[bk][:, 0:512], AF.Silu, scale=sb_sb2[:, 2:3]),
                           reads=(b_psf[bk], b_sb2), writes=(bgb,),
                           tag=f"silu{nt}_{m3}")
                    ssl = sb_st[:, m3 * 2048 + nt * 512: m3 * 2048 + nt * 512 + 512]
                    sch.op("vector", lambda e, gb=gb, pb=pb, ssl=ssl: e.tensor_mul(
                        pb[:], gb[:], ssl),
                           reads=(bgb, b_st[m3]), writes=(bpb,),
                           tag=f"prod{nt}_{m3}")
                    for j in range(4):
                        transp(psf[bk][:, j * 128:(j + 1) * 128],
                               pb[:, j * 128:(j + 1) * 128], sb_idf[:],
                               reads=(bpb, b_const),
                               writes=(b_psf[bk],) if j == 0 else (),
                               pwrites=(b_psf[bk],) if j else (),
                               tag=f"ptr{nt}_{m3}_{j}")
                    for j in range(4):
                        osl = sb_ob[j][:, m3 * 128:(m3 + 1) * 128]
                        sch.op("vector",
                               lambda e, osl=osl, bk=bk, j=j: e.tensor_add(
                                   osl, psf[bk][:, j * 128:(j + 1) * 128], osl),
                               reads=(b_psf[bk], b_ob[j]), pwrites=(b_ob[j],),
                               tag=f"radd{nt}_{m3}_{j}")
            outs.append(em)

        def em_post(nt=nt):
            for j in range(4):
                gt = nt * 4 + j
                dma("sync", dout.ap()[gt * 128:(gt + 1) * 128, :], sb_ob[j][:],
                    reads=(b_ob[j],), pwrites=(b_out,), tag=f"stout{nt}_{j}")
        outs.append(em_post)
        return outs

    pending_f2 = []
    for nt in range(4):
        f0s = f0_round_emitters(nt)
        fi = iter(f0s)
        for em in pending_f2:
            em()
            nxt = next(fi, None)
            if nxt is not None:
                nxt()
        for em in fi:
            em()
        f1_emit(nt)
        if nt == 0 and "t1" in ddbg:
            dma("sync", ddbg["t1"].ap(), sb_t1[:], reads=tuple(b_t1),
                pwrites=(b_dbg,), tag="dbg_t1")
        if nt == 0 and "t2" in ddbg:
            dma("sync", ddbg["t2"].ap(), sb_t2[:], reads=tuple(b_t2),
                pwrites=(b_dbg,), tag="dbg_t2")
        pending_f2 = f2_round_emitters(nt)
    for em in pending_f2:
        em()

    sch.fin([b_out, b_dbg])
    return _emit(nc, sch)


def _emit(nc, sch):
    sch.finalize()
    sems = {}
    for eng in _Sched.ENGINES:
        sems[(eng, None)] = nc.alloc_semaphore(f"sem_{eng}")
        for ln in range(_Sched.NLANE):
            sems[(eng, ln)] = nc.alloc_semaphore(f"sem_{eng}_d{ln}")

    with nc.Block() as block:
        @block.tensor
        def _(e):
            sch.run("tensor", e, sems)

        @block.vector
        def _(e):
            sch.run("vector", e, sems)

        @block.scalar
        def _(e):
            sch.run("scalar", e, sems)

        @block.gpsimd
        def _(e):
            sch.run("gpsimd", e, sems)

        @block.sync
        def _(e):
            sch.run("sync", e, sems)

    return nc


def _prep(L0_attn_query, L0_attn_in_proj_w, L0_attn_out_proj_w,
          L0_ffn_k0, L0_ffn_k1, L0_ffn_k2, L0_ffn_s0, L0_ffn_s1, L0_ffn_s2):
    x = np.ascontiguousarray(np.asarray(L0_attn_query, np.float32))
    win = np.ascontiguousarray(np.asarray(L0_attn_in_proj_w, np.float32))
    wout = np.ascontiguousarray(np.asarray(L0_attn_out_proj_w, np.float32))
    wqk = np.ascontiguousarray(
        win[:2048].T.reshape(8, 128, 16, 128).transpose(2, 1, 0, 3))
    wv = np.ascontiguousarray(win[2048:].T.reshape(8, 128, 1024))
    wo = np.ascontiguousarray(
        wout.T.reshape(8, 128, 8, 128).transpose(2, 1, 0, 3))
    k0 = np.ascontiguousarray(np.asarray(L0_ffn_k0)).view(np.uint8)
    k0a = np.ascontiguousarray(k0[:1024].reshape(8, 128, 4096))
    k0b = np.ascontiguousarray(k0[1024:].reshape(8, 128, 4096))
    k1 = np.ascontiguousarray(np.asarray(L0_ffn_k1)).view(np.uint8)
    k1 = k1.reshape(32, 128, 4096)
    k2 = np.ascontiguousarray(np.asarray(L0_ffn_k2)).view(np.uint8)
    k2 = k2.reshape(32, 128, 1024)
    shared = dict(wqk=wqk, wv=wv, wo=wo, k0a=k0a, k0b=k0b, k1=k1, k2=k2,
                  s0=np.asarray(L0_ffn_s0, np.float32).reshape(1, 1),
                  s1=np.asarray(L0_ffn_s1, np.float32).reshape(1, 1),
                  s2=np.asarray(L0_ffn_s2, np.float32).reshape(1, 1))
    for name, arr in _host_constants().items():
        shared["c_" + name] = np.ascontiguousarray(arr)
    return x, shared


def kernel(L0_attn_query, L0_attn_in_proj_w, L0_attn_out_proj_w,
           L0_ffn_k0, L0_ffn_k1, L0_ffn_k2, L0_ffn_s0, L0_ffn_s1, L0_ffn_s2):
    x, shared = _prep(L0_attn_query, L0_attn_in_proj_w, L0_attn_out_proj_w,
                      L0_ffn_k0, L0_ffn_k1, L0_ffn_k2,
                      L0_ffn_s0, L0_ffn_s1, L0_ffn_s2)
    nc = build_nc()
    in_maps = [dict(shared, x=np.ascontiguousarray(x[c])) for c in range(8)]
    res = run_bass_kernel_spmd(nc, in_maps, core_ids=list(range(8)))
    return np.stack([np.asarray(r["out"], np.float32) for r in res.results],
                    axis=0)


# version 16
# speedup vs baseline: 1.0530x; 1.0530x over previous
import numpy as np
import ml_dtypes
import concourse.bass as bass
import concourse.mybir as mybir
from concourse.bass_utils import run_bass_kernel_spmd

F32 = mybir.dt.float32
BF16 = mybir.dt.bfloat16
U8 = mybir.dt.uint8
AF = mybir.ActivationFunctionType
ALU = mybir.AluOpType
AX = mybir.AxisListType
BFD = ml_dtypes.bfloat16

T_SEQ = 2048
DM = 1024
NH = 16
NTT = 16
NEG_BIG = -1.0e30
N_SLOT = 6

class _Rec:
    __slots__ = ("eng", "dma", "thunk", "deps", "marked", "value", "waits",
                 "seq", "tag", "lane", "lane_pred")

    def __init__(self, eng, dma, thunk, seq, tag):
        self.eng = eng
        self.dma = dma
        self.thunk = thunk
        self.deps = []
        self.marked = False
        self.value = None
        self.waits = None
        self.seq = seq
        self.tag = tag
        self.lane = None
        self.lane_pred = None


class _Buf:
    __slots__ = ("name", "writers", "readers")

    def __init__(self, name):
        self.name = name
        self.writers = []
        self.readers = []

    def absorb(self, others):
        for o in others:
            self.writers = self.writers + o.writers + o.readers


class _Sched:
    ENGINES = ("tensor", "vector", "scalar", "gpsimd", "sync")
    NLANE = 8

    def __init__(self):
        self.recs = {e: [] for e in self.ENGINES}
        self.nseq = 0
        self.dma_count = {e: 0 for e in self.ENGINES}
        self.lane_last = {}

    def op(self, eng, thunk, reads=(), writes=(), pwrites=(), dma=False, tag=""):
        r = _Rec(eng, dma, thunk, self.nseq, tag)
        self.nseq += 1
        if dma:
            lane = self.dma_count[eng] % self.NLANE
            self.dma_count[eng] += 1
            r.lane = lane
            r.lane_pred = self.lane_last.get((eng, lane))
            self.lane_last[(eng, lane)] = r
        deps = {}
        for b in reads:
            for d in b.writers:
                deps[id(d)] = d
        for b in tuple(writes) + tuple(pwrites):
            for d in b.writers:
                deps[id(d)] = d
            for d in b.readers:
                deps[id(d)] = d
        if dma and r.lane_pred is not None:
            deps[id(r.lane_pred)] = r.lane_pred
        for d in deps.values():
            if d is r:
                continue
            if (not d.dma) and (not dma) and d.eng == eng == "tensor":
                continue
            assert d.seq < r.seq, (d.tag, r.tag)
            r.deps.append(d)
        for b in reads:
            b.readers.append(r)
        for b in writes:
            b.writers = [r]
            b.readers = []
        for b in pwrites:
            b.writers.append(r)
        self.recs[eng].append(r)
        return r

    def finalize(self):
        for lst in self.recs.values():
            for r in lst:
                for d in r.deps:
                    d.marked = True
                if r.dma:
                    r.marked = True
        for lst in self.recs.values():
            cnt = {}
            for r in lst:
                if r.marked:
                    key = ("d", r.lane) if r.dma else "c"
                    cnt[key] = cnt.get(key, 0) + 1
                    r.value = cnt[key] * (16 if r.dma else 1)
        for lst in self.recs.values():
            hw = {}
            for r in lst:
                ws = []
                for d in r.deps:
                    key = (d.eng, d.lane) if d.dma else (d.eng, None)
                    assert d.value is not None
                    if hw.get(key, 0) < d.value:
                        ws.append((key, d.value))
                        hw[key] = d.value
                ws.sort(key=lambda kv: kv[1])
                r.waits = ws

    def fin(self, bufs, tag="fin"):
        r = self.op("sync", lambda e: e.nop(), reads=tuple(bufs), tag=tag)
        have = {id(d) for d in r.deps}
        for rec in self.lane_last.values():
            if id(rec) not in have:
                r.deps.append(rec)
                have.add(id(rec))
        return r

    def run(self, eng_name, eng, sems):
        for r in self.recs[eng_name]:
            for key, val in r.waits:
                eng.wait_ge(sems[key], val)
            ins = r.thunk(eng)
            if r.marked:
                assert ins is not None, r.tag
                key = (eng_name, r.lane) if r.dma else (eng_name, None)
                ins.then_inc(sems[key], 16 if r.dma else 1)


def _host_constants():
    c = {}
    c["ident_f32"] = np.eye(128, dtype=np.float32)
    c["ident_bf"] = np.eye(128, dtype=np.float32).astype(BFD)
    i_of_r = (np.arange(128) % 64) // 2
    freqs = np.exp(-np.log(10000.0) * i_of_r / 32.0)
    t = np.arange(T_SEQ, dtype=np.float64)
    ang = freqs[:, None] * t[None, :]
    cos = np.cos(ang)
    sin = np.sin(ang)
    sgn = np.where(np.arange(128) % 2 == 0, -1.0, 1.0)[:, None]
    c["Cq"] = (0.125 * cos).astype(np.float32)
    c["Sq"] = (0.125 * sgn * sin).astype(np.float32)
    c["Ck"] = cos.astype(np.float32)
    c["Sk"] = (sgn * sin).astype(np.float32)
    tq = np.arange(128)[:, None]
    cw = np.arange(384)[None, :]
    valid = (cw > tq + 6) & (cw <= tq + 256)
    c["maskT"] = np.where(valid, 0.0, NEG_BIG).astype(np.float32)
    c["ones_bf"] = np.ones((128, 1), np.float32).astype(BFD)
    c["neg1"] = np.full((128, 1), -1.0, np.float32)
    c["onesc2"] = np.full((1, 128), 2.0, np.float32)
    c["one11"] = np.ones((1, 1), np.float32)
    c["onesc1"] = np.ones((1, 128), np.float32)
    return c


_CONST_SPECS = [
    ("ident_f32", (128, 128), F32), ("ident_bf", (128, 128), BF16),
    ("Cq", (128, 2048), F32), ("Sq", (128, 2048), F32),
    ("Ck", (128, 2048), F32), ("Sk", (128, 2048), F32),
    ("maskT", (128, 384), F32), ("ones_bf", (128, 1), BF16),
    ("neg1", (128, 1), F32),
    ("onesc2", (1, 128), F32), ("one11", (1, 1), F32),
    ("onesc1", (1, 128), F32),
]


def build_nc(debug=(), stage=4):
    nc = bass.Bass("TRN2", target_bir_lowering=False, debug=False)
    sch = _Sched()

    dx = nc.dram_tensor("x", [T_SEQ, DM], F32, kind="ExternalInput")
    dwqk = nc.dram_tensor("wqk", [16, 128, 8, 128], F32, kind="ExternalInput")
    dwv = nc.dram_tensor("wv", [8, 128, 1024], F32, kind="ExternalInput")
    dwo = nc.dram_tensor("wo", [8, 128, 8, 128], F32, kind="ExternalInput")
    dk0a = nc.dram_tensor("k0a", [8, 128, 4096], U8, kind="ExternalInput")
    dk0b = nc.dram_tensor("k0b", [8, 128, 4096], U8, kind="ExternalInput")
    dk1 = nc.dram_tensor("k1", [32, 128, 4096], U8, kind="ExternalInput")
    dk2 = nc.dram_tensor("k2", [32, 128, 1024], U8, kind="ExternalInput")
    dsc = [nc.dram_tensor(f"s{j}", [1, 1], F32, kind="ExternalInput")
           for j in range(3)]
    dconst = {name: nc.dram_tensor("c_" + name, list(shp), dt, kind="ExternalInput")
              for name, shp, dt in _CONST_SPECS}
    dout = nc.dram_tensor("out", [T_SEQ, DM], F32, kind="ExternalOutput")
    dw0s = nc.dram_tensor("w0s", [8, 128, 4096], BF16, kind="Internal")
    dw1s = nc.dram_tensor("w1s", [32, 128, 4096], BF16, kind="Internal")
    dw2s = nc.dram_tensor("w2s", [32, 128, 1024], BF16, kind="Internal")

    ddbg = {}
    for name, shp, dt in [
        ("xn_f", [128, 16384], BF16), ("q_f", [128, 16384], BF16),
        ("k_f", [128, 16384], BF16), ("v_t", [128, 16384], BF16),
        ("a_t", [128, 16384], BF16),
        ("a_f", [128, 16384], BF16), ("s_t", [128, 16384], F32),
        ("t1", [128, 16384], BF16), ("t2", [128, 16384], BF16),
        ("w0s", [8, 128, 4096], BF16), ("w1s", [32, 128, 4096], BF16),
    ]:
        if name in debug:
            ddbg[name] = nc.dram_tensor("dbg_" + name, shp, dt,
                                        kind="ExternalOutput")

    sb_xn = nc.alloc_sbuf_tensor("xn_f", [128, 16384], BF16)
    sb_mask = nc.alloc_sbuf_tensor("maskT", [128, 384], F32)
    sb_idf = nc.alloc_sbuf_tensor("ident_f32", [128, 128], F32)
    sb_idb = nc.alloc_sbuf_tensor("ident_bf", [128, 128], BF16)
    sb_1bf = nc.alloc_sbuf_tensor("ones_bf", [128, 1], BF16)
    sb_neg1 = nc.alloc_sbuf_tensor("neg1", [128, 1], F32)
    sb_oc2 = nc.alloc_sbuf_tensor("onesc2", [1, 128], F32)
    sb_one11 = nc.alloc_sbuf_tensor("one11", [1, 1], F32)
    sb_oc1 = nc.alloc_sbuf_tensor("onesc1", [1, 128], F32)
    sb_sca = nc.alloc_sbuf_tensor("scalars", [1, 4], F32)
    sb_sb2 = nc.alloc_sbuf_tensor("s012b2", [128, 4], F32)
    sb_lnst = nc.alloc_sbuf_tensor("lnst", [128, 12], F32)
    sb_rec = nc.alloc_sbuf_tensor("rec", [128, N_SLOT], F32)
    sb_smx = nc.alloc_sbuf_tensor("softmax_sums", [128, N_SLOT], F32)
    sb_mng = nc.alloc_sbuf_tensor("mneg", [128, N_SLOT], F32)

    sb_st = nc.alloc_sbuf_tensor("s_t", [128, 16384], F32)
    a1 = nc.lookup_mloc(sb_st).addr
    sb_q = nc.alloc_sbuf_tensor_at("q_f", [128, 16384], BF16, offset=a1)
    sb_k = nc.alloc_sbuf_tensor_at("k_f", [128, 16384], BF16, offset=a1 + 32768)
    sb_vt = nc.alloc_sbuf_tensor("v_t", [128, 16384], BF16)
    a2 = nc.lookup_mloc(sb_vt).addr
    sb_af = nc.alloc_sbuf_tensor_at("a_f", [128, 16384], BF16, offset=a2)
    sb_t2 = nc.alloc_sbuf_tensor_at("t2", [128, 16384], BF16, offset=a2)
    sb_at = nc.alloc_sbuf_tensor("a_t", [128, 16384], BF16)
    a3 = nc.lookup_mloc(sb_at).addr
    sb_Cq = nc.alloc_sbuf_tensor_at("Cq", [128, 2048], F32, offset=a3)
    sb_Sq = nc.alloc_sbuf_tensor_at("Sq", [128, 2048], F32, offset=a3 + 8192)
    sb_Ck = nc.alloc_sbuf_tensor_at("Ck", [128, 2048], F32, offset=a3 + 16384)
    sb_Sk = nc.alloc_sbuf_tensor_at("Sk", [128, 2048], F32, offset=a3 + 24576)
    sb_t1 = nc.alloc_sbuf_tensor_at("t1", [128, 16384], BF16, offset=a3)
    R4 = 45056
    sb_r4 = nc.alloc_sbuf_tensor("R4", [128, R4], U8)
    a4 = nc.lookup_mloc(sb_r4).addr

    def r4(name, shape, dtype, off):
        nbytes = int(np.prod(shape[1:])) * mybir.dt.size(dtype)
        assert off + nbytes <= R4, (name, off, nbytes)
        return nc.alloc_sbuf_tensor_at(name, list(shape), dtype, offset=a4 + off)

    sb_xb = [r4(f"xbuf{i}", [128, 1024], F32, i * 4096) for i in range(2)]
    sb_wqs = [r4(f"wqkst{i}", [128, 1024], F32, 8192 + i * 4096) for i in range(2)]
    sb_sqd = r4("sqdump", [128, 1024], F32, 12288)
    sb_wqb = [r4(f"wqkbf{i}", [128, 1024], BF16, 16384 + i * 2048) for i in range(2)]
    sb_wvb = r4("wv_bf", [128, 8192], BF16, 20480)
    sb_wvs = r4("wv_st", [128, 1024], F32, 36864)
    sb_swp = [r4("swp", [128, 512], F32, 40960),
              r4("swp1", [128, 512], F32, 36864)]
    sb_tr1 = [r4("tr1", [128, 512], F32, 43008),
              r4("tr11", [128, 512], F32, 38912)]
    sb_cvA = [r4(f"cvA{i}", [128, 2048], BF16, i * 4096) for i in range(2)]
    sb_cvC = [r4(f"cvC{i}", [128, 2048], BF16, 8192 + i * 4096) for i in range(2)]
    sb_T1 = r4("T1", [128, N_SLOT * 384], F32, 16384)
    sb_Pe = r4("Pe", [128, N_SLOT * 384], BF16, 25600)
    sb_PeT = r4("PeT", [128, N_SLOT * 384], BF16, 30208)
    sb_slab = [r4(f"slab{i}", [128, 6144], BF16, i * 6144) for i in range(3)]
    sb_gb = [r4(f"gbuf{i}", [128, 512], F32, 18432 + i * 2048) for i in range(2)]
    sb_pb = [r4(f"pbuf{i}", [128, 512], F32, 22528 + i * 2048) for i in range(2)]
    sb_ob = [r4(f"outb{i}", [128, 1024], F32, 26624 + i * 4096) for i in range(4)]

    psf = [nc.alloc_psum_tensor(f"psf{i}", [128, 512], F32) for i in range(6)]
    psb = [nc.alloc_psum_tensor(f"psb{i}", [128, 1024], BF16) for i in range(2)]

    B = _Buf
    b_const = B("const")
    b_cs = B("cs_tables")
    b_sca = B("sca")
    b_sb2 = B("sb2")
    b_xb = [B(f"xb{i}") for i in range(2)]
    b_lnst = [B(f"lnst{i}") for i in range(2)]
    b_sqd = B("sqd")
    b_xnf = [[B(f"xnf{c}_{q}") for q in range(4)] for c in range(8)]
    b_wqs = [B(f"wqs{i}") for i in range(2)]
    b_wqb = [B(f"wqb{i}") for i in range(2)]
    b_wvs = B("wvs")
    b_wvb = B("wvb")
    b_q = [B(f"q{c}") for c in range(8)]
    b_k = [B(f"k{c}") for c in range(8)]
    b_vt = [B(f"vt{t}") for t in range(NTT)]
    b_swp = [B("swp0"), B("swp1")]
    b_tr1 = [B("tr10"), B("tr11")]
    b_psf = [B(f"psf{i}") for i in range(6)]
    b_psb = [B(f"psb{i}") for i in range(2)]
    b_T1 = [B(f"T1_{s}") for s in range(N_SLOT)]
    b_Pe = [B(f"Pe_{s}") for s in range(N_SLOT)]
    b_PeT = [B(f"PeT_{s}") for s in range(N_SLOT)]
    b_smx = [B(f"smx{s}") for s in range(N_SLOT)]
    b_mng = [B(f"mng{s}") for s in range(N_SLOT)]
    b_recsb = [B(f"recsb{s}") for s in range(N_SLOT)]
    b_at = [B(f"at{t}") for t in range(NTT)]
    b_af = [B(f"af{c}") for c in range(8)]
    b_st = [B(f"st{t}") for t in range(NTT)]
    b_cvA = [B(f"cvA{i}") for i in range(2)]
    b_cvC = [B(f"cvC{i}") for i in range(2)]
    b_w0s = [B(f"w0s{c}") for c in range(8)]
    b_w1s = [B(f"w1s{c}") for c in range(32)]
    b_w2s = [B(f"w2s{c}") for c in range(32)]

    b_slab = [B(f"slab{i}") for i in range(3)]
    b_t1 = [B(f"t1_{m}") for m in range(32)]
    b_t2 = [B(f"t2_{m}") for m in range(32)]
    b_gb = [B(f"gb{i}") for i in range(2)]
    b_pb = [B(f"pb{i}") for i in range(2)]
    b_ob = [B(f"ob{i}") for i in range(4)]
    b_out = B("out_dram")
    b_dbg = B("dbg")

    def dma(q, dst, src, reads=(), writes=(), pwrites=(), tag=""):
        return sch.op(q, lambda e, d=dst, s=src: e.dma_start(d, s),
                      reads=reads, writes=writes, pwrites=pwrites, dma=True,
                      tag=tag)

    def mm(out, lhsT, rhs, start, stop, reads=(), writes=(), pwrites=(), tag=""):
        return sch.op("tensor",
                      lambda e, o=out, l=lhsT, r=rhs, st=start, sp=stop:
                      e.matmul(o, l, r, start=st, stop=sp),
                      reads=reads, writes=writes, pwrites=pwrites, tag=tag)

    def transp(out, inp, ident, reads=(), writes=(), pwrites=(), tag=""):
        return sch.op("tensor",
                      lambda e, o=out, i=inp, d=ident: e.transpose(o, i, d),
                      reads=reads, writes=writes, pwrites=pwrites, tag=tag)

    sb_konst = dict(ident_f32=sb_idf, ident_bf=sb_idb, Cq=sb_Cq, Sq=sb_Sq,
                    Ck=sb_Ck, Sk=sb_Sk, maskT=sb_mask, ones_bf=sb_1bf,
                    neg1=sb_neg1, onesc2=sb_oc2, one11=sb_one11, onesc1=sb_oc1)

    def ldconst(name):
        buf = b_cs if name[0] in "CS" and name != "maskT" else b_const
        dma("sync", sb_konst[name][:], dconst[name].ap(),
            pwrites=(buf,), tag=f"ld_{name}")

    for t in range(2):
        dma("sync", sb_xb[t][:], dx.ap()[t * 128:(t + 1) * 128, :],
            writes=(b_xb[t],), tag=f"ldx{t}")
    ldconst("ident_f32")
    for kc in range(8):
        dma("sync", sb_wvs[:], dwv.ap()[kc], writes=(b_wvs,), tag=f"ldwv{kc}")
        sch.op("scalar",
               lambda e, kc=kc: e.copy(sb_wvb[:, kc * 1024:(kc + 1) * 1024],
                                       sb_wvs[:]),
               reads=(b_wvs,), pwrites=(b_wvb,), tag=f"cwv{kc}")
    for m in range(2):
        dma("sync", sb_wqs[m][:], dwqk.ap()[m], writes=(b_wqs[m],),
            tag=f"ldwqk{m}")
        sch.op("scalar", lambda e, m=m: e.copy(sb_wqb[m][:], sb_wqs[m][:]),
               reads=(b_wqs[m],), writes=(b_wqb[m],), tag=f"cwqk{m}")
    for name, _, _dt in _CONST_SPECS:
        if name != "ident_f32":
            ldconst(name)
    for j in range(3):
        dma("sync", sb_sca[0:1, j:j + 1], dsc[j].ap(), pwrites=(b_sca,),
            tag=f"ld_s{j}")
    for j in range(3):
        mm(psf[5][:, j:j + 1], sb_oc2[:, :], sb_sca[0:1, j:j + 1], True, True,
           reads=(b_const, b_sca),
           writes=(b_psf[5],) if j == 0 else (), pwrites=(b_psf[5],) if j else (),
           tag=f"bc_s{j}")
    sch.op("vector", lambda e: e.tensor_copy(sb_sb2[:, 0:3], psf[5][:, 0:3]),
           reads=(b_psf[5],), writes=(b_sb2,), tag="ev_s012")
    b_sqd.absorb([b_wqs[1]])

    xnf_r = sb_xn[:].rearrange("p (c n) -> p c n", c=8)
    for t in range(NTT):
        xb = sb_xb[t % 2]
        bxb = b_xb[t % 2]
        sl = t % 2
        bst = b_lnst[sl]
        st_c = sb_lnst[:, sl * 6: sl * 6 + 6]
        xs, mu, vs, v1, ln1, inv = (st_c[:, i:i + 1] for i in range(6))
        if t >= 2:
            dma("sync", xb[:], dx.ap()[t * 128:(t + 1) * 128, :], writes=(bxb,),
                tag=f"ldx{t}")
        sch.op("vector", lambda e, xb=xb, xs=xs: e.reduce_sum(
            xs, xb[:], axis=AX.X),
               reads=(bxb,), writes=(bst,), tag=f"ln_sum{t}")
        sch.op("vector", lambda e, mu=mu, xs=xs: e.tensor_scalar_mul(
            mu, xs, 1.0 / DM),
               reads=(bst,), pwrites=(bst,), tag=f"ln_mu{t}")
        sch.op("vector", lambda e, xb=xb, mu=mu: e.tensor_scalar(
            xb[:], xb[:], mu, None, op0=ALU.subtract),
               reads=(bst,), writes=(bxb,), tag=f"ln_sub{t}")
        sch.op("scalar", lambda e, xb=xb, vs=vs: e.activation(
            sb_sqd[:], xb[:], AF.Square, accum_out=vs),
               reads=(bxb,), writes=(b_sqd,), pwrites=(bst,), tag=f"ln_sq{t}")
        sch.op("vector", lambda e, v1=v1, vs=vs: e.tensor_scalar(
            v1, vs, 1.0 / DM, 1e-5, op0=ALU.mult, op1=ALU.add),
               reads=(bst,), pwrites=(bst,), tag=f"ln_v1{t}")
        sch.op("scalar", lambda e, ln1=ln1, v1=v1: e.activation(
            ln1, v1, AF.Ln),
               reads=(bst,), pwrites=(bst,), tag=f"ln_ln{t}")
        sch.op("scalar", lambda e, inv=inv, ln1=ln1: e.activation(
            inv, ln1, AF.Exp, scale=-0.5),
               reads=(bst,), pwrites=(bst,), tag=f"ln_inv{t}")
        sch.op("vector", lambda e, xb=xb, inv=inv: e.tensor_scalar_mul(
            xb[:], xb[:], inv),
               reads=(bst,), writes=(bxb,), tag=f"ln_fin{t}")
        for g in range(2):
            bank = psf[4 + g]
            bb = b_psf[4 + g]
            for qq in range(4):
                cchunk = g * 4 + qq
                transp(bank[:, qq * 128:(qq + 1) * 128],
                       xb[:, cchunk * 128:(cchunk + 1) * 128], sb_idf[:],
                       reads=(bxb, b_const),
                       writes=(bb,) if qq == 0 else (),
                       pwrites=(bb,) if qq else (),
                       tag=f"tp{t}_{cchunk}")
            dst = xnf_r[:, g * 4:(g + 1) * 4, t * 128:(t + 1) * 128]
            src = bank[:].rearrange("p (c n) -> p c n", c=4)
            sch.op("scalar", lambda e, d=dst, s=src: e.copy(d, s),
                   reads=(bb,),
                   pwrites=tuple(b_xnf[g * 4 + c][t // 4] for c in range(4)),
                   tag=f"tpev{t}_{g}")
    b_wqs[1].absorb([b_sqd])

    swmask = []
    for i in range(16):
        swmask += [2 * i + 1, 2 * i]
    b_swp[1].absorb([b_wvs])
    b_tr1[1].absorb([b_wvs])
    for m in range(16):
        stg, bstg = sb_wqs[m % 2], b_wqs[m % 2]
        wb, bwb = sb_wqb[m % 2], b_wqb[m % 2]
        if m >= 2:
            dma("sync", stg[:], dwqk.ap()[m], writes=(bstg,), tag=f"ldwqk{m}")
            sch.op("scalar", lambda e, wb=wb, stg=stg: e.copy(wb[:], stg[:]),
                   reads=(bstg,), writes=(bwb,), tag=f"cwqk{m}")
        isq = m < 8
        ctab, stab = (sb_Cq, sb_Sq) if isq else (sb_Ck, sb_Sk)
        dstt = sb_q if isq else sb_k
        bdst = b_q[m] if isq else b_k[m - 8]
        chunk = m if isq else m - 8
        for nt in range(4):
            bank, bb = psf[nt], b_psf[nt]
            for kc in range(8):
                mm(bank[:, 0:512], wb[:, kc * 128:(kc + 1) * 128],
                   sb_xn[:, kc * 2048 + nt * 512: kc * 2048 + nt * 512 + 512],
                   kc == 0, kc == 7,
                   reads=(bwb, b_xnf[kc][nt]),
                   writes=(bb,) if kc == 0 else (), pwrites=(bb,) if kc else (),
                   tag=f"qk{m}_{nt}_{kc}")
            cs = slice(nt * 512, nt * 512 + 512)
            rsl = (m * 4 + nt) % 2
            swp, bswp = sb_swp[rsl], b_swp[rsl]
            tr1, btr1 = sb_tr1[rsl], b_tr1[rsl]
            sch.op("vector", lambda e, bank=bank, swp=swp: e.stream_shuffle(
                swp[:], bank[:, 0:512], swmask),
                   reads=(bb,), writes=(bswp,), tag=f"rp_sh{m}_{nt}")
            sch.op("vector", lambda e, bank=bank, ctab=ctab, cs=cs, tr1=tr1:
                   e.tensor_mul(tr1[:], bank[:, 0:512], ctab[:, cs]),
                   reads=(bb, b_cs), writes=(btr1,), tag=f"rp_c{m}_{nt}")
            sch.op("vector", lambda e, stab=stab, cs=cs, swp=swp: e.tensor_mul(
                swp[:], swp[:], stab[:, cs]),
                   reads=(bswp, b_cs), writes=(bswp,), tag=f"rp_s{m}_{nt}")
            dst = dstt[:, chunk * 2048 + nt * 512: chunk * 2048 + nt * 512 + 512]
            sch.op("vector", lambda e, d=dst, tr1=tr1, swp=swp: e.tensor_add(
                d, tr1[:], swp[:]),
                   reads=(btr1, bswp), pwrites=(bdst,), tag=f"rp_o{m}_{nt}")
    for tt in range(NTT):
        for og in range(2):
            bank = psf[4 + ((tt * 2 + og) % 2)]
            bb = b_psf[4 + ((tt * 2 + og) % 2)]
            for kc in range(8):
                mm(bank[:, 0:512],
                   sb_xn[:, kc * 2048 + tt * 128: kc * 2048 + tt * 128 + 128],
                   sb_wvb[:, kc * 1024 + og * 512: kc * 1024 + og * 512 + 512],
                   kc == 0, kc == 7,
                   reads=(b_xnf[kc][tt // 4], b_wvb),
                   writes=(bb,) if kc == 0 else (), pwrites=(bb,) if kc else (),
                   tag=f"v{tt}_{og}_{kc}")
            dst = sb_vt[:, tt * 1024 + og * 512: tt * 1024 + og * 512 + 512]
            sch.op("scalar", lambda e, d=dst, bank=bank: e.copy(d, bank[:, 0:512]),
                   reads=(bb,), pwrites=(b_vt[tt],), tag=f"vev{tt}_{og}")

    if "xn_f" in ddbg:
        dma("sync", ddbg["xn_f"].ap(), sb_xn[:], reads=tuple(b for row in b_xnf for b in row), pwrites=(b_dbg,), tag="dbg_xn")
    if "q_f" in ddbg:
        dma("sync", ddbg["q_f"].ap(), sb_q[:], reads=tuple(b_q), pwrites=(b_dbg,), tag="dbg_q")
    if "k_f" in ddbg:
        dma("sync", ddbg["k_f"].ap(), sb_k[:], reads=tuple(b_k), pwrites=(b_dbg,), tag="dbg_k")
    if "v_t" in ddbg:
        dma("sync", ddbg["v_t"].ap(), sb_vt[:], reads=tuple(b_vt), pwrites=(b_dbg,), tag="dbg_v")

    if stage < 2:
        sch.fin([b_dbg])
        return _emit(nc, sch)
    b_cvA[0].absorb([b_xb[0], b_xb[1]])
    b_cvA[1].absorb([b_xb[1], b_wqs[0]])
    b_cvC[0].absorb([b_wqs[0], b_wqs[1], b_sqd])
    b_cvC[1].absorb([b_wqs[1], b_sqd])
    for s in range(N_SLOT):
        b_T1[s].absorb([b_wqb[0], b_wqb[1], b_wvb])
        b_Pe[s].absorb([b_wvb])
        b_PeT[s].absorb([b_wvb])

    conv_steps = []

    def conv_half(src_ap, dst_ap, wbuf, idx, src2_ap=None, tagn="", n=2048):
        ia, ic = idx % 2, idx % 2

        def s1():
            dma("gpsimd", sb_cvA[ia][:, 0:n], src_ap, writes=(b_cvA[ia],),
                tag=f"cva{tagn}")

        def s1b():
            dma("gpsimd", sb_cvC[ic][:, 0:n], src2_ap, writes=(b_cvC[ic],),
                tag=f"cvb{tagn}")

        def s2sub():
            sch.op("vector", lambda e: e.tensor_sub(
                sb_cvA[ia][:, 0:n], sb_cvA[ia][:, 0:n], sb_cvC[ic][:, 0:n]),
                   reads=(b_cvA[ia], b_cvC[ic]), writes=(b_cvA[ia],),
                   tag=f"cvs{tagn}")

        def s2aff():
            sch.op("scalar", lambda e: e.activation(
                sb_cvA[ia][:, 0:n], sb_cvA[ia][:, 0:n], AF.Identity,
                scale=2.0, bias=sb_neg1[:, 0:1]),
                   reads=(b_cvA[ia], b_const), writes=(b_cvA[ia],),
                   tag=f"cvf{tagn}")

        def s3():
            dma("sync", dst_ap, sb_cvA[ia][:, 0:n], reads=(b_cvA[ia],),
                writes=(wbuf,), tag=f"cvw{tagn}")
        if src2_ap is not None:
            return [s1, s1b, s2sub, s3]
        return [s1, s2aff, s3]

    ci = 0
    for kc in range(8):
        for h in range(2):
            conv_steps.extend(conv_half(
                dk0a.ap()[kc, :, h * 2048:(h + 1) * 2048],
                dw0s.ap()[kc, :, h * 2048:(h + 1) * 2048],
                b_w0s[kc], ci,
                src2_ap=dk0b.ap()[kc, :, h * 2048:(h + 1) * 2048],
                tagn=f"w0_{kc}_{h}"))
            ci += 1
    for kc in range(32):
        for h in range(2):
            conv_steps.extend(conv_half(
                dk1.ap()[kc, :, h * 2048:(h + 1) * 2048],
                dw1s.ap()[kc, :, h * 2048:(h + 1) * 2048],
                b_w1s[kc], ci, tagn=f"w1_{kc}_{h}"))
            ci += 1
    for kc in range(32):
        conv_steps.extend(conv_half(
            dk2.ap()[kc], dw2s.ap()[kc], b_w2s[kc], ci, tagn=f"w2_{kc}", n=1024))
        ci += 1
    conv_iter = iter(conv_steps)

    def emit_conv(k):
        for _ in range(k):
            step = next(conv_iter, None)
            if step is None:
                return
            step()

    for t in range(NTT):
        b_at[t].absorb([b_cs])
    units = [(i, h) for i in range(NTT) for h in range(NH)]
    NU = len(units)

    def uctx(u):
        i, h = units[u]
        slot = u % N_SLOT
        jj0 = max(0, 2 - i)
        return dict(i=i, h=h, slot=slot, bank=psf[slot], bbank=b_psf[slot],
                    pbank=psb[u % 2], bpbank=b_psb[u % 2],
                    c=h // 2, off=(h % 2) * 64, jj0=jj0,
                    js=list(range(jj0, 3)))

    def st_mm(u):
        t = uctx(u)
        i, h, c, off, jj0 = t["i"], t["h"], t["c"], t["off"], t["jj0"]
        w = (3 - jj0) * 128
        k0c = (i - 2 + jj0) * 128
        mm(t["bank"][:, jj0 * 128:384],
           sb_q[off:off + 64, c * 2048 + i * 128: c * 2048 + i * 128 + 128],
           sb_k[off:off + 64, c * 2048 + k0c: c * 2048 + k0c + w],
           True, True, reads=(b_k[c], b_q[c]), writes=(t["bbank"],),
           tag=f"sq{i}_{h}")

    def st_mskmx(u):
        t = uctx(u)
        i, h, slot, jj0 = t["i"], t["h"], t["slot"], t["jj0"]
        t1s = sb_T1[:, slot * 384 + jj0 * 128: slot * 384 + 384]
        sch.op("vector", lambda e, bank=t["bank"], t1s=t1s, jj0=jj0: e.tensor_add(
            t1s, bank[:, jj0 * 128:384], sb_mask[:, jj0 * 128:384]),
               reads=(t["bbank"], b_const), writes=(b_T1[slot],), tag=f"msk{i}_{h}")
        sch.op("vector", lambda e, t1s=t1s, slot=slot: e.reduce_max(
            sb_mng[:, slot:slot + 1], t1s, axis=AX.X, negate=True),
               reads=(b_T1[slot],), writes=(b_mng[slot],), tag=f"mx{i}_{h}")

    def st_exp(u):
        t = uctx(u)
        i, h, slot, jj0 = t["i"], t["h"], t["slot"], t["jj0"]
        t1s = sb_T1[:, slot * 384 + jj0 * 128: slot * 384 + 384]
        pes = sb_Pe[:, slot * 384 + jj0 * 128: slot * 384 + 384]
        sch.op("scalar", lambda e, t1s=t1s, pes=pes, slot=slot: e.activation(
            pes, t1s, AF.Exp, bias=sb_mng[:, slot:slot + 1],
            accum_out=sb_smx[:, slot:slot + 1]),
               reads=(b_T1[slot], b_mng[slot]),
               writes=(b_Pe[slot], b_smx[slot]), tag=f"exp{i}_{h}")

    def st_tp(u):
        t = uctx(u)
        i, h, slot, jj0 = t["i"], t["h"], t["slot"], t["jj0"]
        for jj in t["js"]:
            transp(t["pbank"][:, jj * 128:(jj + 1) * 128],
                   sb_Pe[:, slot * 384 + jj * 128: slot * 384 + jj * 128 + 128],
                   sb_idb[:],
                   reads=(b_Pe[slot], b_const),
                   writes=(t["bpbank"],) if jj == jj0 else (),
                   pwrites=(t["bpbank"],) if jj != jj0 else (),
                   tag=f"pt{i}_{h}_{jj}")

    def st_rcp(u):
        t = uctx(u)
        i, h, slot = t["i"], t["h"], t["slot"]
        sch.op("vector", lambda e, slot=slot: e.reciprocal(
            sb_rec[:, slot:slot + 1], sb_smx[:, slot:slot + 1]),
               reads=(b_smx[slot],), writes=(b_recsb[slot],), tag=f"rcp{i}_{h}")

    def st_ptev(u):
        t = uctx(u)
        i, h, slot, jj0 = t["i"], t["h"], t["slot"], t["jj0"]
        pets = sb_PeT[:, slot * 384 + jj0 * 128: slot * 384 + 384]
        sch.op("scalar", lambda e, pets=pets, pbank=t["pbank"], jj0=jj0: e.copy(
            pets, pbank[:, jj0 * 128:384]),
               reads=(t["bpbank"],), writes=(b_PeT[slot],), tag=f"ptev{i}_{h}")

    def st_av(u):
        t = uctx(u)
        i, h, slot, jj0 = t["i"], t["h"], t["slot"], t["jj0"]
        for jj in t["js"]:
            jt = i - 2 + jj
            mm(t["bank"][:, 384:448],
               sb_PeT[:, slot * 384 + jj * 128: slot * 384 + jj * 128 + 128],
               sb_vt[:, jt * 1024 + h * 64: jt * 1024 + h * 64 + 64],
               jj == jj0, jj == 2,
               reads=(b_PeT[slot], b_vt[jt]), pwrites=(t["bbank"],),
               tag=f"av{i}_{h}_{jj}")

    def st_aev(u):
        t = uctx(u)
        i, h, slot = t["i"], t["h"], t["slot"]
        dst = sb_at[:, i * 1024 + h * 64: i * 1024 + h * 64 + 64]
        sch.op("scalar",
               lambda e, d=dst, bank=t["bank"], slot=slot: e.activation(
                   d, bank[:, 384:448], AF.Copy, scale=sb_rec[:, slot:slot + 1]),
               reads=(t["bbank"], b_recsb[slot]), pwrites=(b_at[i],),
               tag=f"aev{i}_{h}")

    stages = [(6, st_aev), (5, st_av), (4, st_ptev), (3, st_rcp), (3, st_tp),
              (2, st_exp), (1, st_mskmx), (0, st_mm)]
    per_it = len(conv_steps) // (NU + 7) + 1
    for it in range(NU + 7):
        emit_conv(per_it)
        for lag, fn in stages:
            u = it - lag
            if 0 <= u < NU:
                fn(u)
    emit_conv(10 ** 9)

    if "a_t" in ddbg:
        dma("sync", ddbg["a_t"].ap(), sb_at[:], reads=tuple(b_at),
            pwrites=(b_dbg,), tag="dbg_at")
    if stage < 3:
        sch.fin([b_dbg])
        return _emit(nc, sch)

    for c in range(8):
        b_af[c].absorb([b_vt[2 * c], b_vt[2 * c + 1]])
    for c in range(8):
        for half in range(2):
            bank, bb = psb[half], b_psb[half]
            for t8 in range(8):
                tt = half * 8 + t8
                transp(bank[:, t8 * 128:(t8 + 1) * 128],
                       sb_at[:, tt * 1024 + c * 128: tt * 1024 + c * 128 + 128],
                       sb_idb[:],
                       reads=(b_at[tt], b_const),
                       writes=(bb,) if t8 == 0 else (),
                       pwrites=(bb,) if t8 else (),
                       tag=f"atp{c}_{tt}")
            dst = sb_af[:, c * 2048 + half * 1024: c * 2048 + half * 1024 + 1024]
            sch.op("vector", lambda e, d=dst, bank=bank: e.tensor_copy(d, bank[:]),
                   reads=(bb,), pwrites=(b_af[c],), tag=f"atev{c}_{half}")

    if "a_f" in ddbg:
        dma("sync", ddbg["a_f"].ap(), sb_af[:], reads=tuple(b_af), pwrites=(b_dbg,), tag="dbg_af")

    for c8 in range(8):
        b_st[c8].absorb([b_q[c8], b_k[c8]])
    for mo in range(8):
        stg, bstg = sb_wqs[mo % 2], b_wqs[mo % 2]
        wb, bwb = sb_wqb[mo % 2], b_wqb[mo % 2]
        dma("sync", stg[:], dwo.ap()[mo], writes=(bstg,), tag=f"ldwo{mo}")
        sch.op("scalar", lambda e, wb=wb, stg=stg: e.copy(wb[:], stg[:]),
               reads=(bstg,), writes=(bwb,), tag=f"cwo{mo}")
        for nt in range(4):
            bank = psf[(mo * 4 + nt) % 4]
            bb = b_psf[(mo * 4 + nt) % 4]
            for kc in range(8):
                mm(bank[:, 0:512], wb[:, kc * 128:(kc + 1) * 128],
                   sb_af[:, kc * 2048 + nt * 512: kc * 2048 + nt * 512 + 512],
                   kc == 0, kc == 7,
                   reads=(bwb, b_af[kc]),
                   writes=(bb,) if kc == 0 else (), pwrites=(bb,) if kc else (),
                   tag=f"op{mo}_{nt}_{kc}")
            dst = sb_st[:, mo * 2048 + nt * 512: mo * 2048 + nt * 512 + 512]
            sch.op("vector", lambda e, d=dst, bank=bank: e.tensor_copy(
                d, bank[:, 0:512]),
                   reads=(bb,), pwrites=(b_st[mo],), tag=f"opev{mo}_{nt}")

    if "s_t" in ddbg:
        dma("sync", ddbg["s_t"].ap(), sb_st[:], reads=tuple(b_st[:8]), pwrites=(b_dbg,), tag="dbg_s")
    if "w0s" in ddbg:
        dma("sync", ddbg["w0s"].ap(), dw0s.ap(), reads=tuple(b_w0s), pwrites=(b_dbg,), tag="dbg_w0s")
    if "w1s" in ddbg:
        dma("sync", ddbg["w1s"].ap(), dw1s.ap(), reads=tuple(b_w1s), pwrites=(b_dbg,), tag="dbg_w1s")

    if stage < 4:
        sch.fin([b_dbg])
        return _emit(nc, sch)

    for m in range(32):
        b_t1[m].absorb([b_at[m // 2]])
        b_t2[m].absorb([b_af[m // 4]])
    for i2 in range(3):
        b_slab[i2].absorb([b_cvA[0], b_cvA[1], b_cvC[0], b_cvC[1],
                           b_wqs[0], b_wqs[1], b_wqb[0], b_wqb[1]] + b_T1)
    for i2 in range(2):
        b_gb[i2].absorb([b_T1[4], b_T1[5], b_wqb[0], b_wqb[1]])
        b_pb[i2].absorb(b_T1 + b_Pe)
    for i2 in range(4):
        b_ob[i2].absorb(b_Pe + b_PeT + [b_wvs] + b_swp + b_tr1)

    def rounds(total, size):
        out = []
        m0 = 0
        while m0 < total:
            out.append(list(range(m0, min(m0 + size, total))))
            m0 += size
        return out

    slab_i = [0]

    def next_slab():
        slab_i[0] = (slab_i[0] + 1) % 3
        return sb_slab[slab_i[0]], b_slab[slab_i[0]]

    bank_set = [0]

    def next_banks(n):
        bank_set[0] ^= 1
        base = bank_set[0] * 3
        return [base + j for j in range(n)]

    def f0_round_emitters(nt):
        outs = []
        for rnd in rounds(32, 3):
            def em(rnd=rnd, nt=nt):
                w = len(rnd)
                banks = next_banks(w)
                slab, bslab = next_slab()
                src = dw0s.ap()[:, :, rnd[0] * 128: rnd[0] * 128 + w * 128]
                src = src.rearrange("c p f -> p c f")
                dst = slab[:, 0:8 * w * 128].rearrange("p (c f) -> p c f", c=8)
                dma("sync", dst, src, reads=tuple(b_w0s), writes=(bslab,),
                    tag=f"f0ld{nt}_{rnd[0]}")
                for kc in range(8):
                    for widx, m1 in enumerate(rnd):
                        bk = banks[widx]
                        mm(psf[bk][:, 0:512],
                           slab[:, kc * w * 128 + widx * 128:
                                kc * w * 128 + widx * 128 + 128],
                           sb_xn[:, kc * 2048 + nt * 512: kc * 2048 + nt * 512 + 512],
                           kc == 0, kc == 7,
                           reads=(bslab, b_xnf[kc][nt]),
                           writes=(b_psf[bk],) if kc == 0 else (),
                           pwrites=(b_psf[bk],) if kc else (),
                           tag=f"f0mm{nt}_{m1}_{kc}")
                for widx, m1 in enumerate(rnd):
                    bk = banks[widx]
                    dst2 = sb_t1[:, m1 * 512:(m1 + 1) * 512]
                    sch.op("vector", lambda e, d=dst2, bk=bk: e.tensor_scalar(
                        d, psf[bk][:, 0:512], sb_sb2[:, 0:1], 0.0,
                        op0=ALU.mult, op1=ALU.max),
                           reads=(b_psf[bk], b_sb2), writes=(b_t1[m1],),
                           tag=f"f0ev{nt}_{m1}")
            outs.append(em)
        return outs

    def f1_emit(nt):
        for rnd in rounds(32, 3):
            w = len(rnd)
            banks = next_banks(w)
            for g in range(4):
                slab, bslab = next_slab()
                src = dw1s.ap()[g * 8:(g + 1) * 8, :,
                                rnd[0] * 128: rnd[0] * 128 + w * 128]
                src = src.rearrange("c p f -> p c f")
                dst = slab[:, 0:8 * w * 128].rearrange("p (c f) -> p c f", c=8)
                dma("sync", dst, src,
                    reads=tuple(b_w1s[g * 8:(g + 1) * 8]), writes=(bslab,),
                    tag=f"f1ld{nt}_{rnd[0]}_{g}")
                for kci in range(8):
                    kc = g * 8 + kci
                    for widx, m2 in enumerate(rnd):
                        bk = banks[widx]
                        mm(psf[bk][:, 0:512],
                           slab[:, kci * w * 128 + widx * 128:
                                kci * w * 128 + widx * 128 + 128],
                           sb_t1[:, kc * 512:(kc + 1) * 512],
                           kc == 0, kc == 31,
                           reads=(bslab, b_t1[kc]),
                           writes=(b_psf[bk],) if kc == 0 else (),
                           pwrites=(b_psf[bk],) if kc else (),
                           tag=f"f1mm{nt}_{m2}_{kc}")
            for widx, m2 in enumerate(rnd):
                bk = banks[widx]
                dst = sb_t2[:, m2 * 512:(m2 + 1) * 512]
                sch.op("vector", lambda e, d=dst, bk=bk: e.tensor_scalar(
                    d, psf[bk][:, 0:512], sb_sb2[:, 1:2], 0.0,
                    op0=ALU.mult, op1=ALU.max),
                       reads=(b_psf[bk], b_sb2), writes=(b_t2[m2],),
                       tag=f"f1ev{nt}_{m2}")

    def f2_round_emitters(nt):
        outs = []

        def em_pre(nt=nt):
            for j in range(4):
                gt = nt * 4 + j
                dma("sync", sb_ob[j][:], dx.ap()[gt * 128:(gt + 1) * 128, :],
                    writes=(b_ob[j],), tag=f"ldxo{nt}_{j}")
        outs.append(em_pre)
        for rnd in rounds(8, 3):
            def em(rnd=rnd, nt=nt):
                w = len(rnd)
                banks = next_banks(w)
                for g in range(4):
                    slab, bslab = next_slab()
                    src = dw2s.ap()[g * 8:(g + 1) * 8, :,
                                    rnd[0] * 128: rnd[0] * 128 + w * 128]
                    src = src.rearrange("c p f -> p c f")
                    dst = slab[:, 0:8 * w * 128].rearrange("p (c f) -> p c f", c=8)
                    dma("sync", dst, src,
                        reads=tuple(b_w2s[g * 8:(g + 1) * 8]), writes=(bslab,),
                        tag=f"f2ld{nt}_{rnd[0]}_{g}")
                    for kci in range(8):
                        kc = g * 8 + kci
                        for widx, m3 in enumerate(rnd):
                            bk = banks[widx]
                            mm(psf[bk][:, 0:512],
                               slab[:, kci * w * 128 + widx * 128:
                                    kci * w * 128 + widx * 128 + 128],
                               sb_t2[:, kc * 512:(kc + 1) * 512],
                               kc == 0, kc == 31,
                               reads=(bslab, b_t2[kc]),
                               writes=(b_psf[bk],) if kc == 0 else (),
                               pwrites=(b_psf[bk],) if kc else (),
                               tag=f"f2mm{nt}_{m3}_{kc}")
                for widx, m3 in enumerate(rnd):
                    bk = banks[widx]
                    gb, bgb = sb_gb[m3 % 2], b_gb[m3 % 2]
                    pb, bpb = sb_pb[m3 % 2], b_pb[m3 % 2]
                    sch.op("scalar", lambda e, gb=gb, bk=bk: e.activation(
                        gb[:], psf[bk][:, 0:512], AF.Silu, scale=sb_sb2[:, 2:3]),
                           reads=(b_psf[bk], b_sb2), writes=(bgb,),
                           tag=f"silu{nt}_{m3}")
                    ssl = sb_st[:, m3 * 2048 + nt * 512: m3 * 2048 + nt * 512 + 512]
                    sch.op("vector", lambda e, gb=gb, pb=pb, ssl=ssl: e.tensor_mul(
                        pb[:], gb[:], ssl),
                           reads=(bgb, b_st[m3]), writes=(bpb,),
                           tag=f"prod{nt}_{m3}")
                    for j in range(4):
                        transp(psf[bk][:, j * 128:(j + 1) * 128],
                               pb[:, j * 128:(j + 1) * 128], sb_idf[:],
                               reads=(bpb, b_const),
                               writes=(b_psf[bk],) if j == 0 else (),
                               pwrites=(b_psf[bk],) if j else (),
                               tag=f"ptr{nt}_{m3}_{j}")
                    for j in range(4):
                        osl = sb_ob[j][:, m3 * 128:(m3 + 1) * 128]
                        sch.op("vector",
                               lambda e, osl=osl, bk=bk, j=j: e.tensor_add(
                                   osl, psf[bk][:, j * 128:(j + 1) * 128], osl),
                               reads=(b_psf[bk], b_ob[j]), pwrites=(b_ob[j],),
                               tag=f"radd{nt}_{m3}_{j}")
            outs.append(em)

        def em_post(nt=nt):
            for j in range(4):
                gt = nt * 4 + j
                dma("sync", dout.ap()[gt * 128:(gt + 1) * 128, :], sb_ob[j][:],
                    reads=(b_ob[j],), pwrites=(b_out,), tag=f"stout{nt}_{j}")
        outs.append(em_post)
        return outs

    pending_f2 = []
    for nt in range(4):
        f0s = f0_round_emitters(nt)
        fi = iter(f0s)
        for em in pending_f2:
            em()
            nxt = next(fi, None)
            if nxt is not None:
                nxt()
        for em in fi:
            em()
        f1_emit(nt)
        if nt == 0 and "t1" in ddbg:
            dma("sync", ddbg["t1"].ap(), sb_t1[:], reads=tuple(b_t1),
                pwrites=(b_dbg,), tag="dbg_t1")
        if nt == 0 and "t2" in ddbg:
            dma("sync", ddbg["t2"].ap(), sb_t2[:], reads=tuple(b_t2),
                pwrites=(b_dbg,), tag="dbg_t2")
        pending_f2 = f2_round_emitters(nt)
    for em in pending_f2:
        em()

    sch.fin([b_out, b_dbg])
    return _emit(nc, sch)


def _emit(nc, sch):
    sch.finalize()
    sems = {}
    for eng in _Sched.ENGINES:
        sems[(eng, None)] = nc.alloc_semaphore(f"sem_{eng}")
        for ln in range(_Sched.NLANE):
            sems[(eng, ln)] = nc.alloc_semaphore(f"sem_{eng}_d{ln}")

    with nc.Block() as block:
        @block.tensor
        def _(e):
            sch.run("tensor", e, sems)

        @block.vector
        def _(e):
            sch.run("vector", e, sems)

        @block.scalar
        def _(e):
            sch.run("scalar", e, sems)

        @block.gpsimd
        def _(e):
            sch.run("gpsimd", e, sems)

        @block.sync
        def _(e):
            sch.run("sync", e, sems)

    return nc


def _prep(L0_attn_query, L0_attn_in_proj_w, L0_attn_out_proj_w,
          L0_ffn_k0, L0_ffn_k1, L0_ffn_k2, L0_ffn_s0, L0_ffn_s1, L0_ffn_s2):
    x = np.ascontiguousarray(np.asarray(L0_attn_query, np.float32))
    win = np.ascontiguousarray(np.asarray(L0_attn_in_proj_w, np.float32))
    wout = np.ascontiguousarray(np.asarray(L0_attn_out_proj_w, np.float32))
    wqk = np.ascontiguousarray(
        win[:2048].T.reshape(8, 128, 16, 128).transpose(2, 1, 0, 3))
    wv = np.ascontiguousarray(win[2048:].T.reshape(8, 128, 1024))
    wo = np.ascontiguousarray(
        wout.T.reshape(8, 128, 8, 128).transpose(2, 1, 0, 3))
    k0 = np.ascontiguousarray(np.asarray(L0_ffn_k0)).view(np.uint8)
    k0a = np.ascontiguousarray(k0[:1024].reshape(8, 128, 4096))
    k0b = np.ascontiguousarray(k0[1024:].reshape(8, 128, 4096))
    k1 = np.ascontiguousarray(np.asarray(L0_ffn_k1)).view(np.uint8)
    k1 = k1.reshape(32, 128, 4096)
    k2 = np.ascontiguousarray(np.asarray(L0_ffn_k2)).view(np.uint8)
    k2 = k2.reshape(32, 128, 1024)
    shared = dict(wqk=wqk, wv=wv, wo=wo, k0a=k0a, k0b=k0b, k1=k1, k2=k2,
                  s0=np.asarray(L0_ffn_s0, np.float32).reshape(1, 1),
                  s1=np.asarray(L0_ffn_s1, np.float32).reshape(1, 1),
                  s2=np.asarray(L0_ffn_s2, np.float32).reshape(1, 1))
    for name, arr in _host_constants().items():
        shared["c_" + name] = np.ascontiguousarray(arr)
    return x, shared


def kernel(L0_attn_query, L0_attn_in_proj_w, L0_attn_out_proj_w,
           L0_ffn_k0, L0_ffn_k1, L0_ffn_k2, L0_ffn_s0, L0_ffn_s1, L0_ffn_s2):
    x, shared = _prep(L0_attn_query, L0_attn_in_proj_w, L0_attn_out_proj_w,
                      L0_ffn_k0, L0_ffn_k1, L0_ffn_k2,
                      L0_ffn_s0, L0_ffn_s1, L0_ffn_s2)
    nc = build_nc()
    in_maps = [dict(shared, x=np.ascontiguousarray(x[c])) for c in range(8)]
    res = run_bass_kernel_spmd(nc, in_maps, core_ids=list(range(8)))
    return np.stack([np.asarray(r["out"], np.float32) for r in res.results],
                    axis=0)
